# version 03
# baseline (speedup 1.0000x reference)
import math
import numpy as np
import ml_dtypes
import concourse.bass as bass
import concourse.mybir as mybir
from concourse.bass_utils import run_bass_kernel_spmd

F32 = mybir.dt.float32
F16 = mybir.dt.float16
BF16 = mybir.dt.bfloat16
AF = mybir.ActivationFunctionType
ALU = mybir.AluOpType
AX = mybir.AxisListType

E = 4096
NH = 32
NKV = 8
DH = 128
GQ = NH // NKV
EPS = 1e-5
INVSQ = 1.0 / math.sqrt(DH)
LORA_SCALE = 2.0
IDMASK = list(range(32))


class Sem:
    def __init__(self, h):
        self.h = h
        self.n = 0


class Prog:
    def __init__(self):
        self.ops = []

    def op(self, eng, waits, fn, inc=None, inc_by=1):
        incs = []
        if inc is not None:
            inc.n += inc_by
            incs.append((inc, inc_by))
        self.ops.append((eng, [(s, v) for (s, v) in waits if v > 0], fn, incs))

    def replay(self, eng_name, eng):
        waited = {}
        for (e, waits, fn, incs) in self.ops:
            if e != eng_name:
                continue
            for (s, v) in waits:
                if waited.get(id(s), 0) < v:
                    eng.wait_ge(s.h, v)
                    waited[id(s)] = v
            inst = fn(eng)
            for (s, k) in incs:
                inst.then_inc(s.h, k)


def build(T=2048, SL=512, dbg=False, phases=6):
    NS = T // SL
    NT = SL // 128
    nc = bass.Bass("TRN2", target_bir_lowering=False)

    x_d = nc.dram_tensor("x", [T, E], F32, kind="ExternalInput").ap()
    out_d = nc.dram_tensor("out", [T, E], F32, kind="ExternalOutput").ap()
    wq_d = nc.dram_tensor("wq", [NH, 128, E], F32, kind="ExternalInput").ap()
    wk_d = nc.dram_tensor("wk", [NKV, 128, E], F32, kind="ExternalInput").ap()
    wv_d = nc.dram_tensor("wv", [NKV, 128, E], F32, kind="ExternalInput").ap()
    wo_d = nc.dram_tensor("wo", [32, 128, E], F32, kind="ExternalInput").ap()
    wf_d = nc.dram_tensor("wf", [32, 128, E], F32, kind="ExternalInput").ap()
    ab_d = nc.dram_tensor("ab", [128, 1152], F32, kind="ExternalInput").ap()
    bb_d = nc.dram_tensor("bb", [33, E], F32, kind="ExternalInput").ap()
    tabc_d = nc.dram_tensor("tabc", [128, T], F16, kind="ExternalInput").ap()
    tabs_d = nc.dram_tensor("tabs", [128, T], F16, kind="ExternalInput").ap()
    iden_d = nc.dram_tensor("iden", [128, 128], BF16, kind="ExternalInput").ap()
    maskc_d = nc.dram_tensor("maskc", [128, 128], BF16, kind="ExternalInput").ap()
    e4_d = nc.dram_tensor("e4", [4, 32], BF16, kind="ExternalInput").ap()
    ones1_d = nc.dram_tensor("ones1", [1, 128], BF16, kind="ExternalInput").ap()
    onesp_d = nc.dram_tensor("onesp", [128, 1], BF16, kind="ExternalInput").ap()
    if dbg:
        dbg_bx = nc.dram_tensor("dbg_bx", [128, NT * E], BF16, kind="ExternalOutput").ap()
        dbg_bc = nc.dram_tensor("dbg_bc", [128, 32 * SL], BF16, kind="ExternalOutput").ap()
        dbg_bn = nc.dram_tensor("dbg_bn", [128, 32 * SL], BF16, kind="ExternalOutput").ap()
        dbg_k = nc.dram_tensor("dbg_k", [128, NKV * T], BF16, kind="ExternalOutput").ap()
        dbg_v = nc.dram_tensor("dbg_v", [128, (T // 128) * 1024], BF16, kind="ExternalOutput").ap()
        dbg_attn = nc.dram_tensor("dbg_attn", [128, 2 * 2048], BF16, kind="ExternalOutput").ap()

    import contextlib
    with contextlib.ExitStack() as _stk:
        Ksb = _stk.enter_context(nc.sbuf_tensor("Ksb", [128, NKV * T], BF16))
        Vsb = _stk.enter_context(nc.sbuf_tensor("Vsb", [128, (T // 128) * 1024], BF16))
        BX = _stk.enter_context(nc.sbuf_tensor("BX", [128, NT * E], BF16))
        BN = _stk.enter_context(nc.sbuf_tensor("BN", [128, 32 * SL], BF16))
        BC = _stk.enter_context(nc.sbuf_tensor("BC", [128, 32 * SL], BF16))
        attn = _stk.enter_context(nc.sbuf_tensor("attn", [128, 2 * 2048], BF16))
        attnT = _stk.enter_context(nc.sbuf_tensor("attnT", [128, 2 * 256], BF16))
        wpool = _stk.enter_context(nc.sbuf_tensor("wpool", [128, 2 * 2048], BF16))
        tabc = _stk.enter_context(nc.sbuf_tensor("tabc_s", [128, T], F16))
        tabsg = _stk.enter_context(nc.sbuf_tensor("tabsg_s", [128, T], F16))
        qsb = _stk.enter_context(nc.sbuf_tensor("qsb", [128, 1024], BF16))
        rt = _stk.enter_context(nc.sbuf_tensor("rt", [128, 512], F32))
        t1 = _stk.enter_context(nc.sbuf_tensor("t1", [128, 512], F32))
        rbc = _stk.enter_context(nc.sbuf_tensor("rbc", [128, 512], BF16))
        mrbc = _stk.enter_context(nc.sbuf_tensor("mrbc", [128, 512], BF16))
        iden = _stk.enter_context(nc.sbuf_tensor("iden_s", [128, 128], BF16))
        maskc = _stk.enter_context(nc.sbuf_tensor("maskc_s", [128, 128], BF16))
        e4 = _stk.enter_context(nc.sbuf_tensor("e4_s", [4, 32], BF16))
        ones1 = _stk.enter_context(nc.sbuf_tensor("ones1_s", [1, 128], BF16))
        onesp = _stk.enter_context(nc.sbuf_tensor("onesp_s", [128, 1], BF16))
        asb = _stk.enter_context(nc.sbuf_tensor("asb", [128, 1152], BF16))
        bbsb = _stk.enter_context(nc.sbuf_tensor("bbsb", [33, E], BF16))
        Ppsb = _stk.enter_context(nc.sbuf_tensor("Ppsb", [33, 512], BF16))
        gbbsb = _stk.enter_context(nc.sbuf_tensor("gbbsb", [32, 512], BF16))
        gtok = _stk.enter_context(nc.sbuf_tensor("gtok", [128, 16], BF16))
        st2 = _stk.enter_context(nc.sbuf_tensor("st2", [128, 2], F32))
        rstd = _stk.enter_context(nc.sbuf_tensor("rstd", [128, 1], F32))
        negmr = _stk.enter_context(nc.sbuf_tensor("negmr", [128, 1], F32))
        lnt = _stk.enter_context(nc.sbuf_tensor("lnt", [128, 1], F32))
        nmx = _stk.enter_context(nc.sbuf_tensor("nmx", [128, 4], F32))
        lsum = _stk.enter_context(nc.sbuf_tensor("lsum", [128, 4], F32))
        rlsum = _stk.enter_context(nc.sbuf_tensor("rlsum", [128, 4], F32))
        gl1 = _stk.enter_context(nc.sbuf_tensor("gl1", [128, 1], F32))
        gl2 = _stk.enter_context(nc.sbuf_tensor("gl2", [128, 1], F32))
        jkv = _stk.enter_context(nc.sbuf_tensor("jkv", [128, 1], F32))
        ps = _stk.enter_context(nc.psum_tensor("ps", [128, 4096], F32))
        _sW = _stk.enter_context(nc.semaphore("sW"))
        _sX = _stk.enter_context(nc.semaphore("sX"))
        _sO = _stk.enter_context(nc.semaphore("sO"))
        _sT = _stk.enter_context(nc.semaphore("sT"))
        _sV = _stk.enter_context(nc.semaphore("sV"))
        _sS = _stk.enter_context(nc.semaphore("sS"))
        block = _stk.enter_context(nc.Block())
        sW, sX, sO = Sem(_sW), Sem(_sX), Sem(_sO)
        sT, sV, sS = Sem(_sT), Sem(_sV), Sem(_sS)
        P = Prog()
        vfmt = qsb[:].bitcast(BF16)[:, 512:1024]
        glogsb = mrbc[0:4, :]
        xntok = attn[:, 0:4096]
        ysq = rt[:].bitcast(BF16)
        _attn_f32 = attn[:].bitcast(F32)
        rowt = _attn_f32[0:1, 0:512]
        rowa = _attn_f32[0:1, 512:1024]
        rowb = _attn_f32[0:1, 1024:1536]
        rowr = attn[0:1, 3072:3584]
        rowmr = attn[0:1, 3584:4096]

        B = lambda i: ps[:, i * 512:(i + 1) * 512]
        BN_f32 = BN[:].bitcast(F32)

        NSLOT = 2
        slot_guard = [(sT, 0)] * NSLOT
        slot_ready = [0] * NSLOT
        slot_i = [0]

        def stream(src_ap):
            i = slot_i[0] % NSLOT
            slot_i[0] += 1
            g = slot_guard[i]
            P.op("gpsimd", [g], lambda e, s=src_ap, i=i: e.dma_start(
                wpool[:, i * 2048:(i + 1) * 2048], s), inc=sW, inc_by=16)
            slot_ready[i] = sW.n
            return i

        def slot_ap(i):
            return wpool[:, i * 2048:(i + 1) * 2048]

        for dst, src in [(tabc, tabc_d), (tabsg, tabs_d), (iden, iden_d),
                         (maskc, maskc_d), (e4, e4_d), (ones1, ones1_d), (onesp, onesp_d)]:
            P.op("sync", [], lambda e, d=dst, s=src: e.dma_start(d[:], s), inc=sX, inc_by=16)
        P.op("gpsimd", [], lambda e: e.dma_start(asb[:], ab_d), inc=sW, inc_by=16)
        P.op("gpsimd", [], lambda e: e.dma_start(bbsb[:], bb_d), inc=sW, inc_by=16)
        init_w = sW.n
        init_x = sX.n
        P.op("vector", [], lambda e: e.memset(Ppsb[32:33, :], 1.0), inc=sV)

        def ln_tok(src_ap, first_waits, norm_waits):
            P.op("vector", list(first_waits),
                 lambda e, a=src_ap: e.reduce_sum(st2[:, 0:1], a, axis=AX.X), inc=sV)
            v0_ = sV.n
            P.op("scalar", list(first_waits),
                 lambda e, a=src_ap: e.activation(BC[:, 0:4096], a, AF.Square), inc=sS)
            s0_ = sS.n
            P.op("vector", [(sS, s0_)],
                 lambda e: e.reduce_sum(st2[:, 1:2], BC[:, 0:4096], axis=AX.X))
            P.op("vector", [(sV, v0_)],
                 lambda e: e.tensor_scalar_mul(gl2[:], st2[:, 0:1], 1.0 / 4096.0))
            P.op("vector", [], lambda e: e.tensor_scalar_mul(lnt[:], st2[:, 1:2], 1.0 / 4096.0))
            P.op("vector", [], lambda e: e.tensor_tensor(gl1[:], gl2[:], gl2[:], ALU.mult))
            P.op("vector", [], lambda e: e.tensor_sub(lnt[:], lnt[:], gl1[:]))
            P.op("vector", [], lambda e: e.tensor_scalar_add(lnt[:], lnt[:], EPS), inc=sV)
            v1_ = sV.n
            P.op("scalar", [(sV, v1_)], lambda e: e.activation(lnt[:], lnt[:], AF.Ln))
            P.op("scalar", [], lambda e: e.activation(rstd[:], lnt[:], AF.Exp, scale=-0.5),
                 inc=sS)
            s1_ = sS.n
            P.op("vector", [(sS, s1_)], lambda e: e.scalar_tensor_tensor(
                negmr[:], gl2[:], -1.0, rstd[:], ALU.mult, ALU.mult), inc=sV)
            v2_ = sV.n
            P.op("scalar", [(sV, v2_)] + norm_waits,
                 lambda e, a=src_ap: e.activation(
                     xntok[:], a, AF.Identity, bias=negmr[:], scale=rstd[:]), inc=sS)
            return sS.n

        tr_evac = {0: 0, 1: 0}

        def transpose_tile_to_fm(dst_buf, tt, src_ready_s, extra_first=()):
            for fg in range(8):
                bank = fg % 2
                w = list(extra_first) if fg == 0 else []
                if fg == 0:
                    w.append((sS, src_ready_s))
                if tr_evac[bank]:
                    w.append((sV, tr_evac[bank]))
                for k in range(4):
                    f = fg * 4 + k
                    P.op("tensor", w if k == 0 else [],
                         lambda e, f=f, k=k, bank=bank: e.matmul(
                             B(bank)[:, k * 128:(k + 1) * 128],
                             xntok[:, f * 128:(f + 1) * 128], iden[:],
                             start=True, stop=True),
                         inc=sT if k == 3 else None)
                t_done = sT.n
                dst = dst_buf[:].rearrange("p (c t) -> p c t", t=SL)[
                    :, 4 * fg:4 * fg + 4, tt * 128:(tt + 1) * 128]
                P.op("vector", [(sT, t_done)],
                     lambda e, d=dst, bank=bank: e.tensor_copy(
                         d, B(bank)[:].rearrange("p (c t) -> p c t", t=128)), inc=sV)
                tr_evac[bank] = sV.n
            return sT.n

        def rope(dst_ap, bank, tcol, t_ready, war=()):
            P.op("vector", [(sT, t_ready)] + list(war),
                 lambda e: e.stream_shuffle(rt[0:64, :], B(bank)[64:128, :], IDMASK))
            P.op("vector", [], lambda e: e.stream_shuffle(rt[64:128, :], B(bank)[0:64, :],
                                                          IDMASK))
            P.op("vector", [], lambda e: e.tensor_tensor(
                t1[:], B(bank)[:], tabc[:, tcol:tcol + 512], ALU.mult))
            P.op("vector", [], lambda e: e.tensor_tensor(
                rt[:], rt[:], tabsg[:, tcol:tcol + 512], ALU.mult))
            P.op("vector", [], lambda e, d=dst_ap: e.tensor_tensor(d, t1[:], rt[:], ALU.add),
                 inc=sV)
            return sV.n

        def chain32(bank, unit_dram, rhs_buf, first_waits, last_inc=True):
            i1 = stream(unit_dram[:, 0:2048])
            i2 = stream(unit_dram[:, 2048:4096])
            for c in range(32):
                ii, off = (i1, c) if c < 16 else (i2, c - 16)
                w = []
                if c == 0:
                    w = list(first_waits) + [(sW, slot_ready[i1])]
                elif c == 16:
                    w = [(sW, slot_ready[i2])]
                P.op("tensor", w, lambda e, ii=ii, off=off, c=c, bank=bank: e.matmul(
                    B(bank), slot_ap(ii)[:, off * 128:(off + 1) * 128],
                    rhs_buf[:, c * SL:c * SL + 512],
                    start=(c == 0), stop=(c == 31 and last_inc)),
                    inc=sT if (c == 31 and last_inc) else None)
                if c == 15:
                    slot_guard[i1] = (sT, sT.n + 1)
                if c == 31:
                    slot_guard[i2] = (sT, sT.n if last_inc else sT.n + 1)
            return sT.n

        prev_slice_v = 0
        prev_out_o = 0
        ctx_evac_v = 0

        for s in range(NS):
            t0 = s * SL
            nkc = 4 * (s + 1)

            prev_tr_t = 0
            for tt in range(NT):
                g = s * NT + tt
                P.op("gpsimd", [(sV, prev_slice_v)] if tt == 0 else [],
                     lambda e, g=g, tt=tt: e.dma_start(
                         BX[:, tt * E:(tt + 1) * E], x_d[g * 128:(g + 1) * 128, :]),
                     inc=sW, inc_by=16)
                xw = sW.n
                s_xn = ln_tok(BX[:, tt * E:(tt + 1) * E], [(sW, xw)],
                              [(sT, prev_tr_t)] if prev_tr_t else [])
                prev_tr_t = transpose_tile_to_fm(
                    BN, tt, s_xn, extra_first=[(sO, prev_out_o)] if tt == 0 else [])
            xn_ready = sV.n

            kv_rope = {}
            for d in range(NKV):
                bank = 2 + d % 2
                fw = [(sV, xn_ready)] if d == 0 else []
                if d >= 2:
                    fw.append((sV, kv_rope[d - 2]))
                t_k = chain32(bank, wk_d[d], BN, fw)
                kv_rope[d] = rope(Ksb[:, d * T + t0:d * T + t0 + 512], bank, t0, t_k)
            vfm_evac = {}
            vtr_evac = 0
            vtr_t = 0
            for d in range(NKV):
                bank = 2 + d % 2
                fw = [(sV, kv_rope[6 + d % 2] if d < 2 else vfm_evac[d - 2])]
                t_v = chain32(bank, wv_d[d], BN, fw)
                w_ev = [(sT, t_v)]
                if vtr_t:
                    w_ev.append((sT, vtr_t))
                P.op("vector", w_ev, lambda e, bank=bank: e.tensor_copy(vfmt, B(bank)),
                     inc=sV)
                vfm_evac[d] = sV.n
                w0 = [(sV, vfm_evac[d])]
                if vtr_evac:
                    w0.append((sV, vtr_evac))
                for k in range(4):
                    P.op("tensor", w0 if k == 0 else [],
                         lambda e, k=k: e.matmul(
                             B(0)[:, k * 128:(k + 1) * 128],
                             vfmt[:, k * 128:(k + 1) * 128], iden[:],
                             start=True, stop=True),
                         inc=sT if k == 3 else None)
                vtr_t = sT.n
                dst = Vsb[:].rearrange("p (kt dv) -> p kt dv", dv=1024)[
                    :, s * NT:s * NT + 4, d * 128:(d + 1) * 128]
                P.op("vector", [(sT, vtr_t)],
                     lambda e, dst=dst: e.tensor_copy(
                         dst, B(0)[:].rearrange("p (kt dv) -> p kt dv", dv=128)), inc=sV)
                vtr_evac = sV.n

            if phases < 3:
                continue
            exp_done = {}
            scale_done = {}
            tr_read_done = {}
            ctx_pair_t = {}
            scores_last_t = {}
            attnT_guard = [0, 0]
            evacT_last = [0]

            def emit_q(h, war_v):
                return chain32(4, wq_d[h], BN, [(sV, war_v)])

            def emit_scores(h, j, h_rope_v):
                kv = h // GQ
                valid = t0 + (j + 1) * 128
                nch = (valid + 511) // 512
                w = [(sV, h_rope_v)]
                pj = (h, j - 1) if j > 0 else ((h - 1, 3) if h > 0 else None)
                if pj is not None:
                    w.append((sS, exp_done[pj]))
                for kc in range(nch):
                    k1 = min(valid, (kc + 1) * 512)
                    P.op("tensor", w if kc == 0 else [],
                         lambda e, kc=kc, k1=k1, kv=kv, j=j, h=h: e.matmul(
                             ps[:, kc * 512:k1],
                             qsb[:, (h % 2) * 512 + j * 128:(h % 2) * 512 + (j + 1) * 128],
                             Ksb[:, kv * T + kc * 512:kv * T + k1],
                             start=True, stop=True))
                P.op("tensor", [], lambda e, valid=valid: e.matmul(
                    ps[:, valid - 128:valid], iden[:], maskc[:],
                    start=False, stop=True, skip_group_check=True), inc=sT)
                t_sc = sT.n
                scores_last_t[h] = t_sc
                P.op("vector", [(sT, t_sc)], lambda e, valid=valid, j=j: e.reduce_max(
                    nmx[:, j:j + 1], ps[:, 0:valid], axis=AX.X))
                P.op("vector", [], lambda e: e.tensor_copy(jkv[:], rstd[:]))
                P.op("vector", [], lambda e, j=j: e.tensor_scalar_mul(
                    nmx[:, j:j + 1], nmx[:, j:j + 1], -INVSQ), inc=sV)
                v_mx = sV.n
                slot = j % 2
                pair = j // 2
                pkey = (h, pair - 1) if pair > 0 else ((h - 1, 1) if h > 0 else None)
                we = [(sV, v_mx)]
                if pkey is not None and tr_read_done.get(pkey, 0):
                    we.append((sT, tr_read_done[pkey]))
                P.op("scalar", we, lambda e, valid=valid, j=j, slot=slot: e.activation(
                    attn[:, slot * 2048:slot * 2048 + valid], ps[:, 0:valid], AF.Exp,
                    bias=nmx[:, j:j + 1], scale=INVSQ), inc=sS)
                exp_done[(h, j)] = sS.n
                P.op("vector", [(sS, sS.n)], lambda e, valid=valid, j=j, slot=slot:
                     e.reduce_sum(lsum[:, j:j + 1],
                                  attn[:, slot * 2048:slot * 2048 + valid], axis=AX.X))
                P.op("vector", [], lambda e: e.tensor_copy(jkv[:], rstd[:]))
                P.op("vector", [], lambda e, j=j: e.reciprocal(
                    rlsum[:, j:j + 1], lsum[:, j:j + 1]))
                P.op("vector", [], lambda e: e.tensor_copy(jkv[:], rstd[:]))
                P.op("vector", [], lambda e, valid=valid, j=j, slot=slot: e.tensor_scalar_mul(
                    attn[:, slot * 2048:slot * 2048 + valid],
                    attn[:, slot * 2048:slot * 2048 + valid], rlsum[:, j:j + 1]), inc=sV)
                scale_done[(h, j)] = sV.n

            def emit_trctx(h, pair):
                kv = h // GQ
                j0, j1 = 2 * pair, 2 * pair + 1
                for c in range(nkc):
                    par = c % 2
                    sbank = 5 + 2 * par
                    js = [j for j in (j0, j1) if c <= 4 * s + j]
                    if not js:
                        continue
                    w = [(sV, scale_done[(h, j1)])]
                    if evacT_last[0]:
                        w.append((sV, evacT_last[0]))
                    for idx, j in enumerate(js):
                        P.op("tensor", w if idx == 0 else [],
                             lambda e, j=j, c=c, sbank=sbank: e.matmul(
                                 ps[:, sbank * 512 + (j % 2) * 128:
                                    sbank * 512 + (j % 2) * 128 + 128],
                                 attn[:, (j % 2) * 2048 + c * 128:
                                      (j % 2) * 2048 + (c + 1) * 128],
                                 iden[:], start=True, stop=True),
                             inc=sT if idx == len(js) - 1 else None)
                    t_tr = sT.n
                    jlo = js[0] % 2
                    nj = len(js)
                    wv = [(sT, t_tr)]
                    if attnT_guard[par]:
                        wv.append((sT, attnT_guard[par]))
                    P.op("vector", wv, lambda e, par=par, jlo=jlo, nj=nj, sbank=sbank:
                         e.tensor_copy(
                             attnT[:, par * 256 + jlo * 128:par * 256 + (jlo + nj) * 128],
                             ps[:, sbank * 512 + jlo * 128:sbank * 512 + (jlo + nj) * 128]),
                         inc=sV)
                    evacT_last[0] = sV.n
                    wt = [(sV, evacT_last[0])]
                    if c == 0 and pair == 0:
                        wt.append((sV, ctx_evac_v))
                    P.op("tensor", wt, lambda e, c=c, par=par, jlo=jlo, nj=nj, kv=kv,
                         pair=pair: e.matmul(
                             B(6)[:, pair * 256 + jlo * 128:pair * 256 + (jlo + nj) * 128],
                             Vsb[:, c * 1024 + kv * 128:c * 1024 + (kv + 1) * 128],
                             attnT[:, par * 256 + jlo * 128:par * 256 + (jlo + nj) * 128],
                             start=(c == 0), stop=(c == nkc - 1),
                             skip_group_check=True), inc=sT)
                    attnT_guard[par] = sT.n
                tr_read_done[(h, pair)] = sT.n
                ctx_pair_t[(h, pair)] = sT.n

            q_t = emit_q(0, sV.n)
            rope_q = {0: rope(qsb[:, 0:512], 4, t0, q_t)}
            for h in range(NH):
                if h > 0:
                    emit_trctx(h - 1, 1)
                    P.op("vector", [(sT, ctx_pair_t[(h - 1, 1)])],
                         lambda e, h=h: e.tensor_copy(BC[:, (h - 1) * 512:h * 512], B(6)),
                         inc=sV)
                    ctx_evac_v = sV.n
                emit_scores(h, 0, rope_q[h])
                if h + 1 < NH:
                    q_t = emit_q(h + 1, rope_q[h])
                emit_scores(h, 1, rope_q[h])
                if h + 1 < NH:
                    rope_q[h + 1] = rope(
                        qsb[:, ((h + 1) % 2) * 512:((h + 1) % 2) * 512 + 512], 4, t0, q_t,
                        war=[(sT, scores_last_t.get(h - 1, 0))])
                emit_trctx(h, 0)
                emit_scores(h, 2, rope_q[h])
                emit_scores(h, 3, rope_q[h])
            emit_trctx(NH - 1, 1)
            P.op("vector", [(sT, ctx_pair_t[(NH - 1, 1)])],
                 lambda e: e.tensor_copy(BC[:, (NH - 1) * 512:NH * 512], B(6)), inc=sV)
            ctx_evac_v = sV.n

            pp_ready = [0]

            def emit_gates(xn2_v):
                for c in range(32):
                    w = [(sV, xn2_v), (sW, init_w)] if c == 0 else []
                    P.op("tensor", w, lambda e, c=c: e.matmul(
                        ps[0:32, 7 * 512:8 * 512], asb[:, c * 36:c * 36 + 32],
                        BN[:, c * SL:c * SL + 512], start=(c == 0), stop=(c == 31)))
                    P.op("tensor", [], lambda e, c=c: e.matmul(
                        ps[0:4, 3 * 512:4 * 512], asb[:, c * 36 + 32:c * 36 + 36],
                        BN[:, c * SL:c * SL + 512], start=(c == 0), stop=(c == 31)),
                        inc=sT if c == 31 else None)
                t_pg = sT.n
                P.op("vector", [(sT, t_pg)], lambda e: e.tensor_copy(
                    glogsb[:], ps[0:4, 3 * 512:4 * 512]), inc=sV)
                v_gl = sV.n
                for k in range(4):
                    P.op("tensor", [(sV, v_gl), (sX, init_x)] if k == 0 else [],
                         lambda e, k=k: e.matmul(
                             B(5)[:, k * 4:(k + 1) * 4],
                             glogsb[:, k * 128:(k + 1) * 128], iden[0:4, 0:4],
                             start=True, stop=True), inc=sT if k == 3 else None)
                t_gt = sT.n
                for k in range(4):
                    P.op("vector", [(sT, t_gt)] if k == 0 else [],
                         lambda e, k=k: e.reduce_max(
                             gl1[:], B(5)[:, k * 4:(k + 1) * 4], axis=AX.X, negate=True),
                         inc=sV)
                    v_m = sV.n
                    P.op("scalar", [(sV, v_m)], lambda e, k=k: e.activation(
                        gtok[:, k * 4:(k + 1) * 4], B(5)[:, k * 4:(k + 1) * 4], AF.Exp,
                        bias=gl1[:]), inc=sS)
                    s_e = sS.n
                    P.op("vector", [(sS, s_e)], lambda e, k=k: e.reduce_sum(
                        gl2[:], gtok[:, k * 4:(k + 1) * 4], axis=AX.X))
                    P.op("vector", [], lambda e: e.tensor_copy(jkv[:], rstd[:]))
                    P.op("vector", [], lambda e: e.reciprocal(gl1[:], gl2[:]))
                    P.op("vector", [], lambda e: e.tensor_copy(jkv[:], rstd[:]))
                    P.op("vector", [], lambda e, k=k: e.tensor_scalar_mul(
                        gtok[:, k * 4:(k + 1) * 4], gtok[:, k * 4:(k + 1) * 4], gl1[:]),
                        inc=sV)
                v_gn = sV.n
                for k in range(4):
                    P.op("tensor", [(sV, v_gn)] if k == 0 else [],
                         lambda e, k=k: e.matmul(
                             ps[0:4, 2 * 512 + k * 128:2 * 512 + (k + 1) * 128],
                             gtok[:, k * 4:(k + 1) * 4], iden[:],
                             start=True, stop=True), inc=sT if k == 3 else None)
                t_gb = sT.n
                P.op("vector", [(sT, t_gb)], lambda e: e.tensor_scalar_mul(
                    glogsb[:], ps[0:4, 2 * 512:3 * 512], LORA_SCALE), inc=sV)
                v_g2 = sV.n
                P.op("tensor", [(sV, v_g2)], lambda e: e.matmul(
                    ps[0:32, 2 * 512:3 * 512], e4[:], glogsb[:], start=True, stop=True),
                    inc=sT)
                t_gbb = sT.n
                P.op("vector", [(sT, t_gbb)], lambda e: e.tensor_copy(
                    gbbsb[:], ps[0:32, 2 * 512:3 * 512]), inc=sV)
                v_gbb = sV.n
                P.op("vector", [(sV, v_gbb)], lambda e: e.tensor_tensor(
                    Ppsb[0:32, :], ps[0:32, 7 * 512:8 * 512], gbbsb[:], ALU.mult), inc=sV)
                pp_ready[0] = sV.n

            out_wait = {}

            def proj_ln_residual(w_dram, src_buf, y_buf, is_ffn, first_v):
                nonlocal prev_out_o
                yev = {0: 0, 1: 0}
                pend_stats = []

                def emit_stats(oc, v_ysq):
                    P.op("tensor", [(sV, v_ysq)], lambda e, oc=oc: e.matmul(
                        B(2)[0:1, :], onesp[:], y_buf[:, oc * 512:(oc + 1) * 512],
                        start=(oc == 0), stop=(oc == 31)))
                    P.op("tensor", [], lambda e, oc=oc: e.matmul(
                        B(3)[0:1, :], onesp[:], ysq[:, (oc % 2) * 512:(oc % 2) * 512 + 512],
                        start=(oc == 0), stop=(oc == 31)), inc=sT)

                for oc in range(32):
                    bank = oc % 2
                    fw = []
                    if oc == 0:
                        fw.append((sV, first_v))
                    if yev[bank]:
                        fw.append((sS, yev[bank]))
                    t_y = chain32(bank, w_dram[oc], src_buf, fw, last_inc=not is_ffn)
                    if is_ffn:
                        P.op("tensor", [(sV, pp_ready[0])] if oc == 0 else [],
                             lambda e, oc=oc, bank=bank: e.matmul(
                                 B(bank), bbsb[:, oc * 128:(oc + 1) * 128], Ppsb[:],
                                 start=False, stop=True, skip_group_check=True), inc=sT)
                        t_y = sT.n
                    P.op("scalar", [(sT, t_y)], lambda e, oc=oc, bank=bank: e.activation(
                        y_buf[:, oc * 512:(oc + 1) * 512], B(bank), AF.Identity), inc=sS)
                    yev[bank] = sS.n
                    wq_ = [(sS, yev[bank])]
                    P.op("vector", wq_, lambda e, oc=oc: e.tensor_tensor(
                        ysq[:, (oc % 2) * 512:(oc % 2) * 512 + 512],
                        y_buf[:, oc * 512:(oc + 1) * 512],
                        y_buf[:, oc * 512:(oc + 1) * 512], ALU.mult), inc=sV)
                    pend_stats.append((oc, sV.n))
                    if len(pend_stats) >= 2:
                        emit_stats(*pend_stats.pop(0))
                while pend_stats:
                    emit_stats(*pend_stats.pop(0))
                stat_t = sT.n
                P.op("vector", [(sT, stat_t)], lambda e: e.tensor_scalar_mul(
                    rowa, B(2)[0:1, :], 1.0 / 4096.0))
                P.op("vector", [], lambda e: e.tensor_scalar_mul(
                    rowb, B(3)[0:1, :], 1.0 / 4096.0))
                P.op("vector", [], lambda e: e.tensor_tensor(
                    rowt, rowa, rowa, ALU.mult))
                P.op("vector", [], lambda e: e.tensor_sub(rowb, rowb, rowt))
                P.op("vector", [], lambda e: e.tensor_scalar_add(rowb, rowb, EPS),
                     inc=sV)
                v_var = sV.n
                P.op("scalar", [(sV, v_var)], lambda e: e.activation(rowb, rowb, AF.Ln))
                P.op("scalar", [], lambda e: e.activation(rowr, rowb, AF.Exp,
                                                          scale=-0.5), inc=sS)
                s_r = sS.n
                P.op("vector", [(sS, s_r)], lambda e: e.tensor_tensor(
                    rowmr, rowa, rowr, ALU.mult), inc=sV)
                v_mr = sV.n
                P.op("tensor", [(sV, v_mr)], lambda e: e.matmul(
                    B(4), ones1[:], rowr, start=True, stop=True))
                P.op("tensor", [], lambda e: e.matmul(
                    B(5), ones1[:], rowmr, start=True, stop=True), inc=sT)
                t_bc = sT.n
                P.op("vector", [(sT, t_bc)], lambda e: e.tensor_copy(rbc[:], B(4)))
                P.op("vector", [], lambda e: e.tensor_copy(mrbc[:], B(5)), inc=sV)
                v_bc = sV.n
                for oc in range(32):
                    P.op("vector", [(sV, v_bc)] if oc == 0 else [],
                         lambda e, oc=oc: e.tensor_tensor(
                             y_buf[:, oc * 512:(oc + 1) * 512],
                             y_buf[:, oc * 512:(oc + 1) * 512], rbc[:], ALU.mult))
                    P.op("vector", [], lambda e, oc=oc: e.tensor_tensor(
                        y_buf[:, oc * 512:(oc + 1) * 512],
                        y_buf[:, oc * 512:(oc + 1) * 512], mrbc[:], ALU.subtract),
                        inc=sV if oc == 31 else None)
                lnv = sV.n
                ep_ev = {6: 0, 7: 0}
                for tt in range(NT):
                    for ocg in range(8):
                        bank = 6 + (ocg % 2)
                        w = [(sV, lnv)] if (tt == 0 and ocg == 0) else []
                        if ep_ev[bank]:
                            w.append((sV, ep_ev[bank]))
                        for k in range(4):
                            oc = ocg * 4 + k
                            P.op("tensor", w if k == 0 else [],
                                 lambda e, oc=oc, k=k, tt=tt, bank=bank: e.matmul(
                                     B(bank)[:, k * 128:(k + 1) * 128],
                                     y_buf[:, oc * 512 + tt * 128:oc * 512 + tt * 128 + 128],
                                     iden[:], start=True, stop=True),
                                 inc=sT if k == 3 else None)
                        t_ep = sT.n
                        if not is_ffn:
                            dst = BX[:, tt * E + ocg * 512:tt * E + (ocg + 1) * 512]
                            P.op("vector", [(sT, t_ep)], lambda e, d=dst, bank=bank:
                                 e.tensor_tensor(d, B(bank), d, ALU.add), inc=sV)
                        else:
                            stg = BN_f32[:, (tt % 2) * 4096 + ocg * 512:
                                         (tt % 2) * 4096 + (ocg + 1) * 512]
                            xr = BX[:, tt * E + ocg * 512:tt * E + (ocg + 1) * 512]
                            w2 = [(sT, t_ep)]
                            if tt >= 2 and ocg == 0:
                                w2.append((sO, out_wait[tt - 2]))
                            P.op("vector", w2, lambda e, stg=stg, xr=xr, bank=bank:
                                 e.tensor_tensor(stg, B(bank), xr, ALU.add), inc=sV)
                        ep_ev[bank] = sV.n
                    if is_ffn:
                        v_tt = sV.n
                        g = s * NT + tt
                        P.op("sync", [(sV, v_tt)], lambda e, g=g, tt=tt: e.dma_start(
                            out_d[g * 128:(g + 1) * 128, :],
                            BN_f32[:, (tt % 2) * 4096:(tt % 2) * 4096 + 4096]),
                            inc=sO, inc_by=16)
                        out_wait[tt] = sO.n
                if is_ffn:
                    prev_out_o = sO.n

            if phases < 4:
                continue
            proj_ln_residual(wo_d, BC, BN, is_ffn=False, first_v=ctx_evac_v)
            x_prime_v = sV.n

            if phases < 5:
                continue
            prev_tr_t = 0
            for tt in range(NT):
                s_xn = ln_tok(BX[:, tt * E:(tt + 1) * E],
                              [(sV, x_prime_v)] if tt == 0 else [],
                              [(sT, prev_tr_t)] if prev_tr_t else [])
                prev_tr_t = transpose_tile_to_fm(BN, tt, s_xn)
            xn2_ready = sV.n

            if phases < 6:
                continue
            emit_gates(xn2_ready)
            proj_ln_residual(wf_d, BN, BC, is_ffn=True, first_v=pp_ready[0])
            prev_slice_v = sV.n

        if phases < 6:
            P.op("sync", [(sV, sV.n), (sS, sS.n)],
                 lambda e: e.dma_start(out_d[0:128, :], BX[:].bitcast(F32)[:, 0:4096]),
                 inc=sO, inc_by=16)
        if dbg:
            for dst, src in [(dbg_bx, BX[:]),
                             (dbg_bc, BC[:]), (dbg_bn, BN[:]), (dbg_k, Ksb[:]),
                             (dbg_v, Vsb[:]), (dbg_attn, attn[:])]:
                P.op("sync", [(sV, sV.n), (sS, sS.n), (sO, sO.n)],
                     lambda e, d=dst, s_=src: e.dma_start(d, s_), inc=sO, inc_by=16)
        final_o = sO.n

        @block.sync
        def _(e):
            P.replay("sync", e)
            e.wait_ge(_sO, final_o)

        @block.gpsimd
        def _(e):
            P.replay("gpsimd", e)

        @block.tensor
        def _(e):
            P.replay("tensor", e)

        @block.vector
        def _(e):
            P.replay("vector", e)

        @block.scalar
        def _(e):
            P.replay("scalar", e)

    return nc



def _rope_tables(T):
    inv_freq = 1.0 / (500000.0 ** (np.arange(0, 128, 2, dtype=np.float32) / 128.0))
    wavelen = 2.0 * math.pi / inv_freq
    cut_lo = 8192.0 / 4.0
    cut_hi = 8192.0 / 1.0
    inv_freq = np.where(wavelen > cut_hi, inv_freq / 32.0, inv_freq).astype(np.float32)
    smooth = ((8192.0 / wavelen - 1.0) / 3.0).astype(np.float32)
    smoothed = ((1.0 - smooth) * inv_freq / 32.0 + smooth * inv_freq).astype(np.float32)
    is_medium = (~(wavelen < cut_hi)) & (~(wavelen > cut_lo))
    inv_freq = np.where(is_medium, smoothed, inv_freq).astype(np.float32)
    pos = np.arange(T, dtype=np.float32)
    freqs = pos[:, None] * inv_freq[None, :]
    cos = np.cos(freqs).astype(np.float32).T
    sin = np.sin(freqs).astype(np.float32).T
    tabc = np.concatenate([cos, cos], 0).astype(np.float16)
    tabsg = np.concatenate([-sin, sin], 0).astype(np.float16)
    return tabc, tabsg


def _retile(WT):
    I, O = WT.shape
    return np.ascontiguousarray(
        WT.reshape(I // 128, 128, O // 128, 128).transpose(2, 1, 0, 3).reshape(
            O // 128, 128, I))


def prep_inputs(x, Wq, Wk, Wv, Wo, W, b, A, Bm, Wr, T):
    wq = _retile(np.ascontiguousarray(Wq.T))
    wk = _retile(np.ascontiguousarray(Wk.T))
    wv = _retile(np.ascontiguousarray(Wv.T))
    wo = _retile(np.ascontiguousarray(Wo.T))
    wf = _retile(np.ascontiguousarray(W.T))
    A2 = A.reshape(32, 4096)
    AW = np.concatenate([A2, Wr], 0).T
    ab = np.ascontiguousarray(AW.reshape(32, 128, 36).transpose(1, 0, 2).reshape(128, 1152))
    bb = np.ascontiguousarray(
        np.concatenate([Bm.transpose(0, 2, 1).reshape(32, 4096), b[None, :]], 0))
    tabc, tabsg = _rope_tables(T)
    iden = np.eye(128, dtype=np.float32).astype(ml_dtypes.bfloat16)
    maskc = np.triu(np.full((128, 128), -1e38, dtype=np.float32), k=1).astype(
        ml_dtypes.bfloat16)
    e4 = np.zeros((4, 32), dtype=np.float32)
    for i in range(4):
        e4[i, i * 8:(i + 1) * 8] = 1.0
    shared = dict(
        wq=np.ascontiguousarray(wq, np.float32), wk=np.ascontiguousarray(wk, np.float32),
        wv=np.ascontiguousarray(wv, np.float32), wo=np.ascontiguousarray(wo, np.float32),
        wf=np.ascontiguousarray(wf, np.float32), ab=np.ascontiguousarray(ab, np.float32),
        bb=np.ascontiguousarray(bb, np.float32), tabc=tabc, tabs=tabsg,
        iden=iden, maskc=maskc, e4=e4.astype(ml_dtypes.bfloat16),
        ones1=np.ones((1, 128), np.float32).astype(ml_dtypes.bfloat16),
        onesp=np.ones((128, 1), np.float32).astype(ml_dtypes.bfloat16))
    in_maps = []
    for bi in range(x.shape[0]):
        m = dict(shared)
        m["x"] = np.ascontiguousarray(x[bi], np.float32)
        in_maps.append(m)
    return in_maps


def kernel(L0_attn_x, L0_attn_Wq, L0_attn_Wk, L0_attn_Wv, L0_attn_Wo,
           L0_ffn_W, L0_ffn_b, L0_ffn_A, L0_ffn_Bm, L0_ffn_Wr):
    x = np.asarray(L0_attn_x, np.float32)
    Bn, T, _ = x.shape
    in_maps = prep_inputs(
        x, np.asarray(L0_attn_Wq, np.float32), np.asarray(L0_attn_Wk, np.float32),
        np.asarray(L0_attn_Wv, np.float32), np.asarray(L0_attn_Wo, np.float32),
        np.asarray(L0_ffn_W, np.float32), np.asarray(L0_ffn_b, np.float32),
        np.asarray(L0_ffn_A, np.float32), np.asarray(L0_ffn_Bm, np.float32),
        np.asarray(L0_ffn_Wr, np.float32), T)
    nc = build(T=T)
    res = run_bass_kernel_spmd(nc, in_maps, core_ids=list(range(Bn)))
    return np.stack([r["out"] for r in res.results], axis=0).astype(np.float32)


# version 04
# speedup vs baseline: 1.1509x; 1.1509x over previous
import math
import numpy as np
import ml_dtypes
import concourse.bass as bass
import concourse.mybir as mybir
from concourse.bass_utils import run_bass_kernel_spmd

F32 = mybir.dt.float32
F16 = mybir.dt.float16
BF16 = mybir.dt.bfloat16
AF = mybir.ActivationFunctionType
ALU = mybir.AluOpType
AX = mybir.AxisListType

E = 4096
NH = 32
NKV = 8
DH = 128
GQ = NH // NKV
EPS = 1e-5
INVSQ = 1.0 / math.sqrt(DH)
LORA_SCALE = 2.0
IDMASK = list(range(32))


class Sem:
    def __init__(self, h):
        self.h = h
        self.n = 0


class Prog:
    def __init__(self):
        self.ops = []

    def op(self, eng, waits, fn, inc=None, inc_by=1):
        incs = []
        if inc is not None:
            inc.n += inc_by
            incs.append((inc, inc_by))
        self.ops.append((eng, [(s, v) for (s, v) in waits if v > 0], fn, incs))

    def replay(self, eng_name, eng):
        waited = {}
        for (e, waits, fn, incs) in self.ops:
            if e != eng_name:
                continue
            for (s, v) in waits:
                if waited.get(id(s), 0) < v:
                    eng.wait_ge(s.h, v)
                    waited[id(s)] = v
            inst = fn(eng)
            for (s, k) in incs:
                inst.then_inc(s.h, k)


def build(T=2048, SL=512, dbg=False, phases=6):
    NS = T // SL
    NT = SL // 128
    nc = bass.Bass("TRN2", target_bir_lowering=False)

    x_d = nc.dram_tensor("x", [T, E], F32, kind="ExternalInput").ap()
    out_d = nc.dram_tensor("out", [T, E], F32, kind="ExternalOutput").ap()
    wq_d = nc.dram_tensor("wq", [NH, 128, E], F32, kind="ExternalInput").ap()
    wk_d = nc.dram_tensor("wk", [NKV, 128, E], F32, kind="ExternalInput").ap()
    wv_d = nc.dram_tensor("wv", [NKV, 128, E], F32, kind="ExternalInput").ap()
    wo_d = nc.dram_tensor("wo", [32, 128, E], F32, kind="ExternalInput").ap()
    wf_d = nc.dram_tensor("wf", [32, 128, E], F32, kind="ExternalInput").ap()
    ab_d = nc.dram_tensor("ab", [128, 1152], F32, kind="ExternalInput").ap()
    bb_d = nc.dram_tensor("bb", [33, E], F32, kind="ExternalInput").ap()
    tabc_d = nc.dram_tensor("tabc", [128, T], F16, kind="ExternalInput").ap()
    tabs_d = nc.dram_tensor("tabs", [128, T], F16, kind="ExternalInput").ap()
    iden_d = nc.dram_tensor("iden", [128, 128], BF16, kind="ExternalInput").ap()
    maskc_d = nc.dram_tensor("maskc", [128, 128], BF16, kind="ExternalInput").ap()
    e4_d = nc.dram_tensor("e4", [4, 32], BF16, kind="ExternalInput").ap()
    ones1_d = nc.dram_tensor("ones1", [1, 128], BF16, kind="ExternalInput").ap()
    onesp_d = nc.dram_tensor("onesp", [128, 1], BF16, kind="ExternalInput").ap()
    if dbg:
        dbg_bx = nc.dram_tensor("dbg_bx", [128, NT * E], BF16, kind="ExternalOutput").ap()
        dbg_bc = nc.dram_tensor("dbg_bc", [128, 32 * SL], BF16, kind="ExternalOutput").ap()
        dbg_bn = nc.dram_tensor("dbg_bn", [128, 32 * SL], BF16, kind="ExternalOutput").ap()
        dbg_k = nc.dram_tensor("dbg_k", [128, NKV * T], BF16, kind="ExternalOutput").ap()
        dbg_v = nc.dram_tensor("dbg_v", [128, (T // 128) * 1024], BF16, kind="ExternalOutput").ap()
        dbg_attn = nc.dram_tensor("dbg_attn", [128, 2 * 2048], BF16, kind="ExternalOutput").ap()

    import contextlib
    with contextlib.ExitStack() as _stk:
        Ksb = _stk.enter_context(nc.sbuf_tensor("Ksb", [128, NKV * T], BF16))
        Vsb = _stk.enter_context(nc.sbuf_tensor("Vsb", [128, (T // 128) * 1024], BF16))
        BX = _stk.enter_context(nc.sbuf_tensor("BX", [128, NT * E], BF16))
        BN = _stk.enter_context(nc.sbuf_tensor("BN", [128, 32 * SL], BF16))
        BC = _stk.enter_context(nc.sbuf_tensor("BC", [128, 32 * SL], BF16))
        attn = _stk.enter_context(nc.sbuf_tensor("attn", [128, 2 * 2048], BF16))
        attnT = _stk.enter_context(nc.sbuf_tensor("attnT", [128, 2 * 256], BF16))
        wpool = _stk.enter_context(nc.sbuf_tensor("wpool", [128, 3 * 2048], BF16))
        tabc = _stk.enter_context(nc.sbuf_tensor("tabc_s", [128, T], F16))
        tabsg = _stk.enter_context(nc.sbuf_tensor("tabsg_s", [128, T], F16))
        qsb = _stk.enter_context(nc.sbuf_tensor("qsb", [128, 1024], BF16))
        rt = _stk.enter_context(nc.sbuf_tensor("rt", [128, 512], F32))
        t1 = _stk.enter_context(nc.sbuf_tensor("t1", [128, 512], F32))
        iden = _stk.enter_context(nc.sbuf_tensor("iden_s", [128, 128], BF16))
        maskc = _stk.enter_context(nc.sbuf_tensor("maskc_s", [128, 128], BF16))
        e4 = _stk.enter_context(nc.sbuf_tensor("e4_s", [4, 32], BF16))
        ones1 = _stk.enter_context(nc.sbuf_tensor("ones1_s", [1, 128], BF16))
        onesp = _stk.enter_context(nc.sbuf_tensor("onesp_s", [128, 1], BF16))
        asb = _stk.enter_context(nc.sbuf_tensor("asb", [128, 1152], BF16))
        bbsb = _stk.enter_context(nc.sbuf_tensor("bbsb", [33, E], BF16))
        Ppsb = _stk.enter_context(nc.sbuf_tensor("Ppsb", [33, 512], BF16))
        gtok = _stk.enter_context(nc.sbuf_tensor("gtok", [128, 16], BF16))
        st2 = _stk.enter_context(nc.sbuf_tensor("st2", [128, 2], F32))
        rstd = _stk.enter_context(nc.sbuf_tensor("rstd", [128, 1], F32))
        negmr = _stk.enter_context(nc.sbuf_tensor("negmr", [128, 1], F32))
        lnt = _stk.enter_context(nc.sbuf_tensor("lnt", [128, 1], F32))
        nmx = _stk.enter_context(nc.sbuf_tensor("nmx", [128, 4], F32))
        lsum = _stk.enter_context(nc.sbuf_tensor("lsum", [128, 4], F32))
        rlsum = _stk.enter_context(nc.sbuf_tensor("rlsum", [128, 4], F32))
        gl1 = _stk.enter_context(nc.sbuf_tensor("gl1", [128, 1], F32))
        gl2 = _stk.enter_context(nc.sbuf_tensor("gl2", [128, 1], F32))
        jkv = _stk.enter_context(nc.sbuf_tensor("jkv", [128, 1], F32))
        ps = _stk.enter_context(nc.psum_tensor("ps", [128, 4096], F32))
        _sW = _stk.enter_context(nc.semaphore("sW"))
        _sX = _stk.enter_context(nc.semaphore("sX"))
        _sO = _stk.enter_context(nc.semaphore("sO"))
        _sT = _stk.enter_context(nc.semaphore("sT"))
        _sV = _stk.enter_context(nc.semaphore("sV"))
        _sS = _stk.enter_context(nc.semaphore("sS"))
        block = _stk.enter_context(nc.Block())
        sW, sX, sO = Sem(_sW), Sem(_sX), Sem(_sO)
        sT, sV, sS = Sem(_sT), Sem(_sV), Sem(_sS)
        P = Prog()
        vfmt = qsb[:].bitcast(BF16)[:, 512:1024]
        rbc = qsb[:, 0:512]
        mrbc = qsb[:, 512:1024]
        glogsb = mrbc[0:4, :]
        gbbsb = rbc[0:32, :]
        xntok = attn[:, 0:4096]
        ysq = rt[:].bitcast(BF16)
        _attn_f32 = attn[:].bitcast(F32)
        rowt = _attn_f32[0:1, 0:512]
        rowa = _attn_f32[0:1, 512:1024]
        rowb = _attn_f32[0:1, 1024:1536]
        rowr = attn[0:1, 3072:3584]
        rowmr = attn[0:1, 3584:4096]

        B = lambda i: ps[:, i * 512:(i + 1) * 512]
        BN_f32 = BN[:].bitcast(F32)

        NSLOT = 3
        slot_guard = [(sT, 0)] * NSLOT
        slot_ready = [0] * NSLOT
        slot_i = [0]

        def stream(src_ap):
            i = slot_i[0] % NSLOT
            slot_i[0] += 1
            g = slot_guard[i]
            P.op("gpsimd", [g], lambda e, s=src_ap, i=i: e.dma_start(
                wpool[:, i * 2048:(i + 1) * 2048], s), inc=sW, inc_by=16)
            slot_ready[i] = sW.n
            return i

        def slot_ap(i):
            return wpool[:, i * 2048:(i + 1) * 2048]

        for dst, src in [(tabc, tabc_d), (tabsg, tabs_d), (iden, iden_d),
                         (maskc, maskc_d), (e4, e4_d), (ones1, ones1_d), (onesp, onesp_d)]:
            P.op("sync", [], lambda e, d=dst, s=src: e.dma_start(d[:], s), inc=sX, inc_by=16)
        P.op("gpsimd", [], lambda e: e.dma_start(asb[:], ab_d), inc=sW, inc_by=16)
        P.op("gpsimd", [], lambda e: e.dma_start(bbsb[:], bb_d), inc=sW, inc_by=16)
        init_w = sW.n
        init_x = sX.n
        P.op("vector", [], lambda e: e.memset(Ppsb[32:33, :], 1.0), inc=sV)

        def ln_tok(src_ap, first_waits, norm_waits):
            P.op("vector", list(first_waits),
                 lambda e, a=src_ap: e.reduce_sum(st2[:, 0:1], a, axis=AX.X), inc=sV)
            v0_ = sV.n
            P.op("scalar", list(first_waits),
                 lambda e, a=src_ap: e.activation(BC[:, 0:4096], a, AF.Square), inc=sS)
            s0_ = sS.n
            P.op("vector", [(sS, s0_)],
                 lambda e: e.reduce_sum(st2[:, 1:2], BC[:, 0:4096], axis=AX.X))
            P.op("vector", [(sV, v0_)],
                 lambda e: e.tensor_scalar_mul(gl2[:], st2[:, 0:1], 1.0 / 4096.0))
            P.op("vector", [], lambda e: e.tensor_scalar_mul(lnt[:], st2[:, 1:2], 1.0 / 4096.0))
            P.op("vector", [], lambda e: e.tensor_tensor(gl1[:], gl2[:], gl2[:], ALU.mult))
            P.op("vector", [], lambda e: e.tensor_sub(lnt[:], lnt[:], gl1[:]))
            P.op("vector", [], lambda e: e.tensor_scalar_add(lnt[:], lnt[:], EPS), inc=sV)
            v1_ = sV.n
            P.op("scalar", [(sV, v1_)], lambda e: e.activation(lnt[:], lnt[:], AF.Ln))
            P.op("scalar", [], lambda e: e.activation(rstd[:], lnt[:], AF.Exp, scale=-0.5),
                 inc=sS)
            s1_ = sS.n
            P.op("vector", [(sS, s1_)], lambda e: e.scalar_tensor_tensor(
                negmr[:], gl2[:], -1.0, rstd[:], ALU.mult, ALU.mult), inc=sV)
            v2_ = sV.n
            P.op("scalar", [(sV, v2_)] + norm_waits,
                 lambda e, a=src_ap: e.activation(
                     xntok[:], a, AF.Identity, bias=negmr[:], scale=rstd[:]), inc=sS)
            return sS.n

        tr_evac = {0: 0, 1: 0}

        def transpose_tile_to_fm(dst_buf, tt, src_ready_s, extra_first=()):
            for fg in range(8):
                bank = fg % 2
                w = list(extra_first) if fg == 0 else []
                if fg == 0:
                    w.append((sS, src_ready_s))
                if tr_evac[bank]:
                    w.append((sV, tr_evac[bank]))
                for k in range(4):
                    f = fg * 4 + k
                    P.op("tensor", w if k == 0 else [],
                         lambda e, f=f, k=k, bank=bank: e.matmul(
                             B(bank)[:, k * 128:(k + 1) * 128],
                             xntok[:, f * 128:(f + 1) * 128], iden[:],
                             start=True, stop=True),
                         inc=sT if k == 3 else None)
                t_done = sT.n
                dst = dst_buf[:].rearrange("p (c t) -> p c t", t=SL)[
                    :, 4 * fg:4 * fg + 4, tt * 128:(tt + 1) * 128]
                P.op("vector", [(sT, t_done)],
                     lambda e, d=dst, bank=bank: e.tensor_copy(
                         d, B(bank)[:].rearrange("p (c t) -> p c t", t=128)), inc=sV)
                tr_evac[bank] = sV.n
            return sT.n

        def rope(dst_ap, bank, tcol, t_ready, war=()):
            P.op("vector", [(sT, t_ready)] + list(war),
                 lambda e: e.stream_shuffle(rt[0:64, :], B(bank)[64:128, :], IDMASK))
            P.op("vector", [], lambda e: e.stream_shuffle(rt[64:128, :], B(bank)[0:64, :],
                                                          IDMASK))
            P.op("vector", [], lambda e: e.tensor_tensor(
                t1[:], B(bank)[:], tabc[:, tcol:tcol + 512], ALU.mult))
            P.op("vector", [], lambda e: e.tensor_tensor(
                rt[:], rt[:], tabsg[:, tcol:tcol + 512], ALU.mult))
            P.op("vector", [], lambda e, d=dst_ap: e.tensor_tensor(d, t1[:], rt[:], ALU.add),
                 inc=sV)
            return sV.n

        def chain32(bank, unit_dram, rhs_buf, first_waits, last_inc=True):
            i1 = stream(unit_dram[:, 0:2048])
            i2 = stream(unit_dram[:, 2048:4096])
            for c in range(32):
                ii, off = (i1, c) if c < 16 else (i2, c - 16)
                w = []
                if c == 0:
                    w = list(first_waits) + [(sW, slot_ready[i1])]
                elif c == 16:
                    w = [(sW, slot_ready[i2])]
                P.op("tensor", w, lambda e, ii=ii, off=off, c=c, bank=bank: e.matmul(
                    B(bank), slot_ap(ii)[:, off * 128:(off + 1) * 128],
                    rhs_buf[:, c * SL:c * SL + 512],
                    start=(c == 0), stop=(c == 31 and last_inc)),
                    inc=sT if (c == 31 and last_inc) else None)
                if c == 15:
                    slot_guard[i1] = (sT, sT.n + 1)
                if c == 31:
                    slot_guard[i2] = (sT, sT.n if last_inc else sT.n + 1)
            return sT.n

        prev_slice_v = 0
        prev_out_o = 0
        ctx_evac_v = 0

        for s in range(NS):
            t0 = s * SL
            nkc = 4 * (s + 1)

            prev_tr_t = 0
            for tt in range(NT):
                g = s * NT + tt
                P.op("gpsimd", [(sV, prev_slice_v)] if tt == 0 else [],
                     lambda e, g=g, tt=tt: e.dma_start(
                         BX[:, tt * E:(tt + 1) * E], x_d[g * 128:(g + 1) * 128, :]),
                     inc=sW, inc_by=16)
                xw = sW.n
                s_xn = ln_tok(BX[:, tt * E:(tt + 1) * E], [(sW, xw)],
                              [(sT, prev_tr_t)] if prev_tr_t else [])
                prev_tr_t = transpose_tile_to_fm(
                    BN, tt, s_xn, extra_first=[(sO, prev_out_o)] if tt == 0 else [])
            xn_ready = sV.n

            kv_rope = {}
            for d in range(NKV):
                bank = 2 + d % 2
                fw = [(sV, xn_ready)] if d == 0 else []
                if d >= 2:
                    fw.append((sV, kv_rope[d - 2]))
                t_k = chain32(bank, wk_d[d], BN, fw)
                kv_rope[d] = rope(Ksb[:, d * T + t0:d * T + t0 + 512], bank, t0, t_k)
            vfm_evac = {}
            vtr_evac = 0
            vtr_t = 0
            for d in range(NKV):
                bank = 2 + d % 2
                fw = [(sV, kv_rope[6 + d % 2] if d < 2 else vfm_evac[d - 2])]
                t_v = chain32(bank, wv_d[d], BN, fw)
                w_ev = [(sT, t_v)]
                if vtr_t:
                    w_ev.append((sT, vtr_t))
                P.op("vector", w_ev, lambda e, bank=bank: e.tensor_copy(vfmt, B(bank)),
                     inc=sV)
                vfm_evac[d] = sV.n
                w0 = [(sV, vfm_evac[d])]
                if vtr_evac:
                    w0.append((sV, vtr_evac))
                for k in range(4):
                    P.op("tensor", w0 if k == 0 else [],
                         lambda e, k=k: e.matmul(
                             B(0)[:, k * 128:(k + 1) * 128],
                             vfmt[:, k * 128:(k + 1) * 128], iden[:],
                             start=True, stop=True),
                         inc=sT if k == 3 else None)
                vtr_t = sT.n
                dst = Vsb[:].rearrange("p (kt dv) -> p kt dv", dv=1024)[
                    :, s * NT:s * NT + 4, d * 128:(d + 1) * 128]
                P.op("vector", [(sT, vtr_t)],
                     lambda e, dst=dst: e.tensor_copy(
                         dst, B(0)[:].rearrange("p (kt dv) -> p kt dv", dv=128)), inc=sV)
                vtr_evac = sV.n

            if phases < 3:
                continue
            exp_done = {}
            scale_done = {}
            tr_read_done = {}
            ctx_pair_t = {}
            scores_last_t = {}
            attnT_guard = [0, 0]
            evacT_last = [0]

            def emit_q(h, war_v):
                return chain32(4, wq_d[h], BN, [(sV, war_v)])

            def emit_scores(h, j, h_rope_v):
                kv = h // GQ
                valid = t0 + (j + 1) * 128
                nch = (valid + 511) // 512
                w = [(sV, h_rope_v)]
                pj = (h, j - 1) if j > 0 else ((h - 1, 3) if h > 0 else None)
                if pj is not None:
                    w.append((sS, exp_done[pj]))
                for kc in range(nch):
                    k1 = min(valid, (kc + 1) * 512)
                    P.op("tensor", w if kc == 0 else [],
                         lambda e, kc=kc, k1=k1, kv=kv, j=j, h=h: e.matmul(
                             ps[:, kc * 512:k1],
                             qsb[:, (h % 2) * 512 + j * 128:(h % 2) * 512 + (j + 1) * 128],
                             Ksb[:, kv * T + kc * 512:kv * T + k1],
                             start=True, stop=True))
                P.op("tensor", [], lambda e, valid=valid: e.matmul(
                    ps[:, valid - 128:valid], iden[:], maskc[:],
                    start=False, stop=True, skip_group_check=True), inc=sT)
                t_sc = sT.n
                scores_last_t[h] = t_sc
                P.op("vector", [(sT, t_sc)], lambda e, valid=valid, j=j: e.reduce_max(
                    nmx[:, j:j + 1], ps[:, 0:valid], axis=AX.X))
                P.op("vector", [], lambda e: e.tensor_copy(jkv[:], rstd[:]))
                P.op("vector", [], lambda e, j=j: e.tensor_scalar_mul(
                    nmx[:, j:j + 1], nmx[:, j:j + 1], -INVSQ), inc=sV)
                v_mx = sV.n
                slot = j % 2
                pair = j // 2
                pkey = (h, pair - 1) if pair > 0 else ((h - 1, 1) if h > 0 else None)
                we = [(sV, v_mx)]
                if pkey is not None and tr_read_done.get(pkey, 0):
                    we.append((sT, tr_read_done[pkey]))
                P.op("scalar", we, lambda e, valid=valid, j=j, slot=slot: e.activation(
                    attn[:, slot * 2048:slot * 2048 + valid], ps[:, 0:valid], AF.Exp,
                    bias=nmx[:, j:j + 1], scale=INVSQ), inc=sS)
                exp_done[(h, j)] = sS.n
                P.op("vector", [(sS, sS.n)], lambda e, valid=valid, j=j, slot=slot:
                     e.reduce_sum(lsum[:, j:j + 1],
                                  attn[:, slot * 2048:slot * 2048 + valid], axis=AX.X))
                P.op("vector", [], lambda e: e.tensor_copy(jkv[:], rstd[:]))
                P.op("vector", [], lambda e, j=j: e.reciprocal(
                    rlsum[:, j:j + 1], lsum[:, j:j + 1]))
                P.op("vector", [], lambda e: e.tensor_copy(jkv[:], rstd[:]))
                P.op("vector", [], lambda e, valid=valid, j=j, slot=slot: e.tensor_scalar_mul(
                    attn[:, slot * 2048:slot * 2048 + valid],
                    attn[:, slot * 2048:slot * 2048 + valid], rlsum[:, j:j + 1]), inc=sV)
                scale_done[(h, j)] = sV.n

            def emit_trctx(h, pair):
                kv = h // GQ
                j0, j1 = 2 * pair, 2 * pair + 1
                for c in range(nkc):
                    par = c % 2
                    sbank = 5 + 2 * par
                    js = [j for j in (j0, j1) if c <= 4 * s + j]
                    if not js:
                        continue
                    w = [(sV, scale_done[(h, j1)])]
                    if evacT_last[0]:
                        w.append((sV, evacT_last[0]))
                    for idx, j in enumerate(js):
                        P.op("tensor", w if idx == 0 else [],
                             lambda e, j=j, c=c, sbank=sbank: e.matmul(
                                 ps[:, sbank * 512 + (j % 2) * 128:
                                    sbank * 512 + (j % 2) * 128 + 128],
                                 attn[:, (j % 2) * 2048 + c * 128:
                                      (j % 2) * 2048 + (c + 1) * 128],
                                 iden[:], start=True, stop=True),
                             inc=sT if idx == len(js) - 1 else None)
                    t_tr = sT.n
                    jlo = js[0] % 2
                    nj = len(js)
                    wv = [(sT, t_tr)]
                    if attnT_guard[par]:
                        wv.append((sT, attnT_guard[par]))
                    P.op("vector", wv, lambda e, par=par, jlo=jlo, nj=nj, sbank=sbank:
                         e.tensor_copy(
                             attnT[:, par * 256 + jlo * 128:par * 256 + (jlo + nj) * 128],
                             ps[:, sbank * 512 + jlo * 128:sbank * 512 + (jlo + nj) * 128]),
                         inc=sV)
                    evacT_last[0] = sV.n
                    wt = [(sV, evacT_last[0])]
                    if c == 0 and pair == 0:
                        wt.append((sV, ctx_evac_v))
                    P.op("tensor", wt, lambda e, c=c, par=par, jlo=jlo, nj=nj, kv=kv,
                         pair=pair: e.matmul(
                             B(6)[:, pair * 256 + jlo * 128:pair * 256 + (jlo + nj) * 128],
                             Vsb[:, c * 1024 + kv * 128:c * 1024 + (kv + 1) * 128],
                             attnT[:, par * 256 + jlo * 128:par * 256 + (jlo + nj) * 128],
                             start=(c == 0), stop=(c == nkc - 1),
                             skip_group_check=True), inc=sT)
                    attnT_guard[par] = sT.n
                tr_read_done[(h, pair)] = sT.n
                ctx_pair_t[(h, pair)] = sT.n

            q_t = emit_q(0, sV.n)
            rope_q = {0: rope(qsb[:, 0:512], 4, t0, q_t)}
            for h in range(NH):
                if h > 0:
                    emit_trctx(h - 1, 1)
                    P.op("vector", [(sT, ctx_pair_t[(h - 1, 1)])],
                         lambda e, h=h: e.tensor_copy(BC[:, (h - 1) * 512:h * 512], B(6)),
                         inc=sV)
                    ctx_evac_v = sV.n
                emit_scores(h, 0, rope_q[h])
                if h + 1 < NH:
                    q_t = emit_q(h + 1, rope_q[h])
                emit_scores(h, 1, rope_q[h])
                if h + 1 < NH:
                    rope_q[h + 1] = rope(
                        qsb[:, ((h + 1) % 2) * 512:((h + 1) % 2) * 512 + 512], 4, t0, q_t,
                        war=[(sT, scores_last_t.get(h - 1, 0))])
                emit_trctx(h, 0)
                emit_scores(h, 2, rope_q[h])
                emit_scores(h, 3, rope_q[h])
            emit_trctx(NH - 1, 1)
            P.op("vector", [(sT, ctx_pair_t[(NH - 1, 1)])],
                 lambda e: e.tensor_copy(BC[:, (NH - 1) * 512:NH * 512], B(6)), inc=sV)
            ctx_evac_v = sV.n

            pp_ready = [0]

            def emit_gates(xn2_v):
                for c in range(32):
                    w = [(sV, xn2_v), (sW, init_w)] if c == 0 else []
                    P.op("tensor", w, lambda e, c=c: e.matmul(
                        ps[0:32, 7 * 512:8 * 512], asb[:, c * 36:c * 36 + 32],
                        BN[:, c * SL:c * SL + 512], start=(c == 0), stop=(c == 31)))
                    P.op("tensor", [], lambda e, c=c: e.matmul(
                        ps[0:4, 3 * 512:4 * 512], asb[:, c * 36 + 32:c * 36 + 36],
                        BN[:, c * SL:c * SL + 512], start=(c == 0), stop=(c == 31)),
                        inc=sT if c == 31 else None)
                t_pg = sT.n
                P.op("vector", [(sT, t_pg)], lambda e: e.tensor_copy(
                    glogsb[:], ps[0:4, 3 * 512:4 * 512]), inc=sV)
                v_gl = sV.n
                for k in range(4):
                    P.op("tensor", [(sV, v_gl), (sX, init_x)] if k == 0 else [],
                         lambda e, k=k: e.matmul(
                             B(5)[:, k * 4:(k + 1) * 4],
                             glogsb[:, k * 128:(k + 1) * 128], iden[0:4, 0:4],
                             start=True, stop=True), inc=sT if k == 3 else None)
                t_gt = sT.n
                for k in range(4):
                    P.op("vector", [(sT, t_gt)] if k == 0 else [],
                         lambda e, k=k: e.reduce_max(
                             gl1[:], B(5)[:, k * 4:(k + 1) * 4], axis=AX.X, negate=True),
                         inc=sV)
                    v_m = sV.n
                    P.op("scalar", [(sV, v_m)], lambda e, k=k: e.activation(
                        gtok[:, k * 4:(k + 1) * 4], B(5)[:, k * 4:(k + 1) * 4], AF.Exp,
                        bias=gl1[:]), inc=sS)
                    s_e = sS.n
                    P.op("vector", [(sS, s_e)], lambda e, k=k: e.reduce_sum(
                        gl2[:], gtok[:, k * 4:(k + 1) * 4], axis=AX.X))
                    P.op("vector", [], lambda e: e.tensor_copy(jkv[:], rstd[:]))
                    P.op("vector", [], lambda e: e.reciprocal(gl1[:], gl2[:]))
                    P.op("vector", [], lambda e: e.tensor_copy(jkv[:], rstd[:]))
                    P.op("vector", [], lambda e, k=k: e.tensor_scalar_mul(
                        gtok[:, k * 4:(k + 1) * 4], gtok[:, k * 4:(k + 1) * 4], gl1[:]),
                        inc=sV)
                v_gn = sV.n
                for k in range(4):
                    P.op("tensor", [(sV, v_gn)] if k == 0 else [],
                         lambda e, k=k: e.matmul(
                             ps[0:4, 2 * 512 + k * 128:2 * 512 + (k + 1) * 128],
                             gtok[:, k * 4:(k + 1) * 4], iden[:],
                             start=True, stop=True), inc=sT if k == 3 else None)
                t_gb = sT.n
                P.op("vector", [(sT, t_gb)], lambda e: e.tensor_scalar_mul(
                    glogsb[:], ps[0:4, 2 * 512:3 * 512], LORA_SCALE), inc=sV)
                v_g2 = sV.n
                P.op("tensor", [(sV, v_g2)], lambda e: e.matmul(
                    ps[0:32, 2 * 512:3 * 512], e4[:], glogsb[:], start=True, stop=True),
                    inc=sT)
                t_gbb = sT.n
                P.op("vector", [(sT, t_gbb)], lambda e: e.tensor_copy(
                    gbbsb[:], ps[0:32, 2 * 512:3 * 512]), inc=sV)
                v_gbb = sV.n
                P.op("vector", [(sV, v_gbb)], lambda e: e.tensor_tensor(
                    Ppsb[0:32, :], ps[0:32, 7 * 512:8 * 512], gbbsb[:], ALU.mult), inc=sV)
                pp_ready[0] = sV.n

            out_wait = {}

            def proj_ln_residual(w_dram, src_buf, y_buf, is_ffn, first_v):
                nonlocal prev_out_o
                yev = {0: 0, 1: 0}
                pend_stats = []

                def emit_stats(oc, v_ysq):
                    P.op("tensor", [(sV, v_ysq)], lambda e, oc=oc: e.matmul(
                        B(2)[0:1, :], onesp[:], y_buf[:, oc * 512:(oc + 1) * 512],
                        start=(oc == 0), stop=(oc == 31)))
                    P.op("tensor", [], lambda e, oc=oc: e.matmul(
                        B(3)[0:1, :], onesp[:], ysq[:, (oc % 2) * 512:(oc % 2) * 512 + 512],
                        start=(oc == 0), stop=(oc == 31)), inc=sT)

                for oc in range(32):
                    bank = oc % 2
                    fw = []
                    if oc == 0:
                        fw.append((sV, first_v))
                    if yev[bank]:
                        fw.append((sS, yev[bank]))
                    t_y = chain32(bank, w_dram[oc], src_buf, fw, last_inc=not is_ffn)
                    if is_ffn:
                        P.op("tensor", [(sV, pp_ready[0])] if oc == 0 else [],
                             lambda e, oc=oc, bank=bank: e.matmul(
                                 B(bank), bbsb[:, oc * 128:(oc + 1) * 128], Ppsb[:],
                                 start=False, stop=True, skip_group_check=True), inc=sT)
                        t_y = sT.n
                    P.op("scalar", [(sT, t_y)], lambda e, oc=oc, bank=bank: e.activation(
                        y_buf[:, oc * 512:(oc + 1) * 512], B(bank), AF.Identity), inc=sS)
                    yev[bank] = sS.n
                    wq_ = [(sS, yev[bank])]
                    P.op("vector", wq_, lambda e, oc=oc: e.tensor_tensor(
                        ysq[:, (oc % 2) * 512:(oc % 2) * 512 + 512],
                        y_buf[:, oc * 512:(oc + 1) * 512],
                        y_buf[:, oc * 512:(oc + 1) * 512], ALU.mult), inc=sV)
                    pend_stats.append((oc, sV.n))
                    if len(pend_stats) >= 2:
                        emit_stats(*pend_stats.pop(0))
                while pend_stats:
                    emit_stats(*pend_stats.pop(0))
                stat_t = sT.n
                P.op("vector", [(sT, stat_t)], lambda e: e.tensor_scalar_mul(
                    rowa, B(2)[0:1, :], 1.0 / 4096.0))
                P.op("vector", [], lambda e: e.tensor_scalar_mul(
                    rowb, B(3)[0:1, :], 1.0 / 4096.0))
                P.op("vector", [], lambda e: e.tensor_tensor(
                    rowt, rowa, rowa, ALU.mult))
                P.op("vector", [], lambda e: e.tensor_sub(rowb, rowb, rowt))
                P.op("vector", [], lambda e: e.tensor_scalar_add(rowb, rowb, EPS),
                     inc=sV)
                v_var = sV.n
                P.op("scalar", [(sV, v_var)], lambda e: e.activation(rowb, rowb, AF.Ln))
                P.op("scalar", [], lambda e: e.activation(rowr, rowb, AF.Exp,
                                                          scale=-0.5), inc=sS)
                s_r = sS.n
                P.op("vector", [(sS, s_r)], lambda e: e.tensor_tensor(
                    rowmr, rowa, rowr, ALU.mult), inc=sV)
                v_mr = sV.n
                P.op("tensor", [(sV, v_mr)], lambda e: e.matmul(
                    B(4), ones1[:], rowr, start=True, stop=True))
                P.op("tensor", [], lambda e: e.matmul(
                    B(5), ones1[:], rowmr, start=True, stop=True), inc=sT)
                t_bc = sT.n
                P.op("vector", [(sT, t_bc)], lambda e: e.tensor_copy(rbc[:], B(4)))
                P.op("vector", [], lambda e: e.tensor_copy(mrbc[:], B(5)), inc=sV)
                v_bc = sV.n
                ln_done = {}
                for ocg in range(8):
                    for k in range(4):
                        oc = ocg * 4 + k
                        P.op("vector", [(sV, v_bc)] if oc == 0 else [],
                             lambda e, oc=oc: e.tensor_tensor(
                                 y_buf[:, oc * 512:(oc + 1) * 512],
                                 y_buf[:, oc * 512:(oc + 1) * 512], rbc[:], ALU.mult))
                        P.op("vector", [], lambda e, oc=oc: e.tensor_tensor(
                            y_buf[:, oc * 512:(oc + 1) * 512],
                            y_buf[:, oc * 512:(oc + 1) * 512], mrbc[:], ALU.subtract),
                            inc=sV if k == 3 else None)
                    ln_done[ocg] = sV.n
                ep_ev = {6: 0, 7: 0}
                for tt in range(NT):
                    for ocg in range(8):
                        bank = 6 + (ocg % 2)
                        w = [(sV, ln_done[ocg])] if tt == 0 else []
                        if ep_ev[bank]:
                            w.append((sV, ep_ev[bank]))
                        for k in range(4):
                            oc = ocg * 4 + k
                            P.op("tensor", w if k == 0 else [],
                                 lambda e, oc=oc, k=k, tt=tt, bank=bank: e.matmul(
                                     B(bank)[:, k * 128:(k + 1) * 128],
                                     y_buf[:, oc * 512 + tt * 128:oc * 512 + tt * 128 + 128],
                                     iden[:], start=True, stop=True),
                                 inc=sT if k == 3 else None)
                        t_ep = sT.n
                        if not is_ffn:
                            dst = BX[:, tt * E + ocg * 512:tt * E + (ocg + 1) * 512]
                            P.op("vector", [(sT, t_ep)], lambda e, d=dst, bank=bank:
                                 e.tensor_tensor(d, B(bank), d, ALU.add), inc=sV)
                        else:
                            stg = BN_f32[:, (tt % 2) * 4096 + ocg * 512:
                                         (tt % 2) * 4096 + (ocg + 1) * 512]
                            xr = BX[:, tt * E + ocg * 512:tt * E + (ocg + 1) * 512]
                            w2 = [(sT, t_ep)]
                            if tt >= 2 and ocg == 0:
                                w2.append((sO, out_wait[tt - 2]))
                            P.op("vector", w2, lambda e, stg=stg, xr=xr, bank=bank:
                                 e.tensor_tensor(stg, B(bank), xr, ALU.add), inc=sV)
                        ep_ev[bank] = sV.n
                    if is_ffn:
                        v_tt = sV.n
                        g = s * NT + tt
                        P.op("sync", [(sV, v_tt)], lambda e, g=g, tt=tt: e.dma_start(
                            out_d[g * 128:(g + 1) * 128, :],
                            BN_f32[:, (tt % 2) * 4096:(tt % 2) * 4096 + 4096]),
                            inc=sO, inc_by=16)
                        out_wait[tt] = sO.n
                if is_ffn:
                    prev_out_o = sO.n

            if phases < 4:
                continue
            proj_ln_residual(wo_d, BC, BN, is_ffn=False, first_v=ctx_evac_v)
            x_prime_v = sV.n

            if phases < 5:
                continue
            prev_tr_t = 0
            for tt in range(NT):
                s_xn = ln_tok(BX[:, tt * E:(tt + 1) * E],
                              [(sV, x_prime_v)] if tt == 0 else [],
                              [(sT, prev_tr_t)] if prev_tr_t else [])
                prev_tr_t = transpose_tile_to_fm(BN, tt, s_xn)
            xn2_ready = sV.n

            if phases < 6:
                continue
            emit_gates(xn2_ready)
            proj_ln_residual(wf_d, BN, BC, is_ffn=True, first_v=pp_ready[0])
            prev_slice_v = sV.n

        if phases < 6:
            P.op("sync", [(sV, sV.n), (sS, sS.n)],
                 lambda e: e.dma_start(out_d[0:128, :], BX[:].bitcast(F32)[:, 0:4096]),
                 inc=sO, inc_by=16)
        if dbg:
            for dst, src in [(dbg_bx, BX[:]),
                             (dbg_bc, BC[:]), (dbg_bn, BN[:]), (dbg_k, Ksb[:]),
                             (dbg_v, Vsb[:]), (dbg_attn, attn[:])]:
                P.op("sync", [(sV, sV.n), (sS, sS.n), (sO, sO.n)],
                     lambda e, d=dst, s_=src: e.dma_start(d, s_), inc=sO, inc_by=16)
        final_o = sO.n

        @block.sync
        def _(e):
            P.replay("sync", e)
            e.wait_ge(_sO, final_o)

        @block.gpsimd
        def _(e):
            P.replay("gpsimd", e)

        @block.tensor
        def _(e):
            P.replay("tensor", e)

        @block.vector
        def _(e):
            P.replay("vector", e)

        @block.scalar
        def _(e):
            P.replay("scalar", e)

    return nc



def _rope_tables(T):
    inv_freq = 1.0 / (500000.0 ** (np.arange(0, 128, 2, dtype=np.float32) / 128.0))
    wavelen = 2.0 * math.pi / inv_freq
    cut_lo = 8192.0 / 4.0
    cut_hi = 8192.0 / 1.0
    inv_freq = np.where(wavelen > cut_hi, inv_freq / 32.0, inv_freq).astype(np.float32)
    smooth = ((8192.0 / wavelen - 1.0) / 3.0).astype(np.float32)
    smoothed = ((1.0 - smooth) * inv_freq / 32.0 + smooth * inv_freq).astype(np.float32)
    is_medium = (~(wavelen < cut_hi)) & (~(wavelen > cut_lo))
    inv_freq = np.where(is_medium, smoothed, inv_freq).astype(np.float32)
    pos = np.arange(T, dtype=np.float32)
    freqs = pos[:, None] * inv_freq[None, :]
    cos = np.cos(freqs).astype(np.float32).T
    sin = np.sin(freqs).astype(np.float32).T
    tabc = np.concatenate([cos, cos], 0).astype(np.float16)
    tabsg = np.concatenate([-sin, sin], 0).astype(np.float16)
    return tabc, tabsg


def _retile(WT):
    I, O = WT.shape
    return np.ascontiguousarray(
        WT.reshape(I // 128, 128, O // 128, 128).transpose(2, 1, 0, 3).reshape(
            O // 128, 128, I))


def prep_inputs(x, Wq, Wk, Wv, Wo, W, b, A, Bm, Wr, T):
    wq = _retile(np.ascontiguousarray(Wq.T))
    wk = _retile(np.ascontiguousarray(Wk.T))
    wv = _retile(np.ascontiguousarray(Wv.T))
    wo = _retile(np.ascontiguousarray(Wo.T))
    wf = _retile(np.ascontiguousarray(W.T))
    A2 = A.reshape(32, 4096)
    AW = np.concatenate([A2, Wr], 0).T
    ab = np.ascontiguousarray(AW.reshape(32, 128, 36).transpose(1, 0, 2).reshape(128, 1152))
    bb = np.ascontiguousarray(
        np.concatenate([Bm.transpose(0, 2, 1).reshape(32, 4096), b[None, :]], 0))
    tabc, tabsg = _rope_tables(T)
    iden = np.eye(128, dtype=np.float32).astype(ml_dtypes.bfloat16)
    maskc = np.triu(np.full((128, 128), -1e38, dtype=np.float32), k=1).astype(
        ml_dtypes.bfloat16)
    e4 = np.zeros((4, 32), dtype=np.float32)
    for i in range(4):
        e4[i, i * 8:(i + 1) * 8] = 1.0
    shared = dict(
        wq=np.ascontiguousarray(wq, np.float32), wk=np.ascontiguousarray(wk, np.float32),
        wv=np.ascontiguousarray(wv, np.float32), wo=np.ascontiguousarray(wo, np.float32),
        wf=np.ascontiguousarray(wf, np.float32), ab=np.ascontiguousarray(ab, np.float32),
        bb=np.ascontiguousarray(bb, np.float32), tabc=tabc, tabs=tabsg,
        iden=iden, maskc=maskc, e4=e4.astype(ml_dtypes.bfloat16),
        ones1=np.ones((1, 128), np.float32).astype(ml_dtypes.bfloat16),
        onesp=np.ones((128, 1), np.float32).astype(ml_dtypes.bfloat16))
    in_maps = []
    for bi in range(x.shape[0]):
        m = dict(shared)
        m["x"] = np.ascontiguousarray(x[bi], np.float32)
        in_maps.append(m)
    return in_maps


def kernel(L0_attn_x, L0_attn_Wq, L0_attn_Wk, L0_attn_Wv, L0_attn_Wo,
           L0_ffn_W, L0_ffn_b, L0_ffn_A, L0_ffn_Bm, L0_ffn_Wr):
    x = np.asarray(L0_attn_x, np.float32)
    Bn, T, _ = x.shape
    in_maps = prep_inputs(
        x, np.asarray(L0_attn_Wq, np.float32), np.asarray(L0_attn_Wk, np.float32),
        np.asarray(L0_attn_Wv, np.float32), np.asarray(L0_attn_Wo, np.float32),
        np.asarray(L0_ffn_W, np.float32), np.asarray(L0_ffn_b, np.float32),
        np.asarray(L0_ffn_A, np.float32), np.asarray(L0_ffn_Bm, np.float32),
        np.asarray(L0_ffn_Wr, np.float32), T)
    nc = build(T=T)
    res = run_bass_kernel_spmd(nc, in_maps, core_ids=list(range(Bn)))
    return np.stack([r["out"] for r in res.results], axis=0).astype(np.float32)


# version 05
# speedup vs baseline: 1.2053x; 1.0473x over previous
import math
import numpy as np
import ml_dtypes
import concourse.bass as bass
import concourse.mybir as mybir
from concourse.bass_utils import run_bass_kernel_spmd

F32 = mybir.dt.float32
F16 = mybir.dt.float16
BF16 = mybir.dt.bfloat16
AF = mybir.ActivationFunctionType
ALU = mybir.AluOpType
AX = mybir.AxisListType

E = 4096
NH = 32
NKV = 8
DH = 128
GQ = NH // NKV
EPS = 1e-5
INVSQ = 1.0 / math.sqrt(DH)
LORA_SCALE = 2.0
IDMASK = list(range(32))


class Sem:
    def __init__(self, h):
        self.h = h
        self.n = 0


class Prog:
    def __init__(self):
        self.ops = []

    def op(self, eng, waits, fn, inc=None, inc_by=1):
        incs = []
        if inc is not None:
            inc.n += inc_by
            incs.append((inc, inc_by))
        self.ops.append((eng, [(s, v) for (s, v) in waits if v > 0], fn, incs))

    def replay(self, eng_name, eng):
        waited = {}
        for (e, waits, fn, incs) in self.ops:
            if e != eng_name:
                continue
            for (s, v) in waits:
                if waited.get(id(s), 0) < v:
                    eng.wait_ge(s.h, v)
                    waited[id(s)] = v
            inst = fn(eng)
            for (s, k) in incs:
                inst.then_inc(s.h, k)


def build(T=2048, SL=512, dbg=False, phases=6):
    NS = T // SL
    NT = SL // 128
    nc = bass.Bass("TRN2", target_bir_lowering=False)

    x_d = nc.dram_tensor("x", [T, E], F32, kind="ExternalInput").ap()
    out_d = nc.dram_tensor("out", [T, E], F32, kind="ExternalOutput").ap()
    wq_d = nc.dram_tensor("wq", [NH, 128, E], F32, kind="ExternalInput").ap()
    wk_d = nc.dram_tensor("wk", [NKV, 128, E], F32, kind="ExternalInput").ap()
    wv_d = nc.dram_tensor("wv", [32, 128, 1024], F32, kind="ExternalInput").ap()
    wo_d = nc.dram_tensor("wo", [32, 128, E], F32, kind="ExternalInput").ap()
    wf_d = nc.dram_tensor("wf", [32, 128, E], F32, kind="ExternalInput").ap()
    ab_d = nc.dram_tensor("ab", [128, 1152], F32, kind="ExternalInput").ap()
    bb_d = nc.dram_tensor("bb", [33, E], F32, kind="ExternalInput").ap()
    tabc_d = nc.dram_tensor("tabc", [128, T], F16, kind="ExternalInput").ap()
    tabs_d = nc.dram_tensor("tabs", [128, T], F16, kind="ExternalInput").ap()
    iden_d = nc.dram_tensor("iden", [128, 128], BF16, kind="ExternalInput").ap()
    maskc_d = nc.dram_tensor("maskc", [128, 128], BF16, kind="ExternalInput").ap()
    maskr_d = nc.dram_tensor("maskr", [128, 128], BF16, kind="ExternalInput").ap()
    e4_d = nc.dram_tensor("e4", [4, 32], BF16, kind="ExternalInput").ap()
    ones1_d = nc.dram_tensor("ones1", [1, 128], BF16, kind="ExternalInput").ap()
    onesp_d = nc.dram_tensor("onesp", [128, 1], BF16, kind="ExternalInput").ap()
    import contextlib
    with contextlib.ExitStack() as _stk:
        Ksb = _stk.enter_context(nc.sbuf_tensor("Ksb", [128, NKV * T], BF16))
        Vsb = _stk.enter_context(nc.sbuf_tensor("Vsb", [128, (T // 128) * 1024], BF16))
        BX = _stk.enter_context(nc.sbuf_tensor("BX", [128, NT * E], BF16))
        BN = _stk.enter_context(nc.sbuf_tensor("BN", [128, 32 * SL], BF16))
        BC = _stk.enter_context(nc.sbuf_tensor("BC", [128, 32 * SL], BF16))
        attn = _stk.enter_context(nc.sbuf_tensor("attn", [128, 2 * 2048], BF16))
        attnT = _stk.enter_context(nc.sbuf_tensor("attnT", [128, 3 * 512], BF16))
        wpool = _stk.enter_context(nc.sbuf_tensor("wpool", [128, 3 * 2048], BF16))
        vslot = _stk.enter_context(nc.sbuf_tensor("vslot", [128, 2 * 1024], BF16))
        tabc = _stk.enter_context(nc.sbuf_tensor("tabc_s", [128, 512], F16))
        tabsg = _stk.enter_context(nc.sbuf_tensor("tabsg_s", [128, 512], F16))
        qsb = _stk.enter_context(nc.sbuf_tensor("qsb", [128, 1024], BF16))
        rt = _stk.enter_context(nc.sbuf_tensor("rt", [128, 512], F32))
        t1 = _stk.enter_context(nc.sbuf_tensor("t1", [128, 512], F32))
        iden = _stk.enter_context(nc.sbuf_tensor("iden_s", [128, 128], BF16))
        maskc = _stk.enter_context(nc.sbuf_tensor("maskc_s", [128, 128], BF16))
        maskr = _stk.enter_context(nc.sbuf_tensor("maskr_s", [128, 128], BF16))
        emf = _stk.enter_context(nc.sbuf_tensor("emf", [128, 4], BF16))
        e4 = _stk.enter_context(nc.sbuf_tensor("e4_s", [4, 32], BF16))
        ones1 = _stk.enter_context(nc.sbuf_tensor("ones1_s", [1, 128], BF16))
        onesp = _stk.enter_context(nc.sbuf_tensor("onesp_s", [128, 1], BF16))
        asb = _stk.enter_context(nc.sbuf_tensor("asb", [128, 1152], BF16))
        bbsb = _stk.enter_context(nc.sbuf_tensor("bbsb", [33, E], BF16))
        Ppsb = _stk.enter_context(nc.sbuf_tensor("Ppsb", [33, 512], BF16))
        gtok = _stk.enter_context(nc.sbuf_tensor("gtok", [128, 16], BF16))
        st2 = _stk.enter_context(nc.sbuf_tensor("st2", [128, 2], F32))
        rstd = _stk.enter_context(nc.sbuf_tensor("rstd", [128, 1], F32))
        negmr = _stk.enter_context(nc.sbuf_tensor("negmr", [128, 1], F32))
        lnt = _stk.enter_context(nc.sbuf_tensor("lnt", [128, 1], F32))
        nmx = _stk.enter_context(nc.sbuf_tensor("nmx", [128, 4], F32))
        lsum = _stk.enter_context(nc.sbuf_tensor("lsum", [128, 4], F32))
        rlsum = _stk.enter_context(nc.sbuf_tensor("rlsum", [128, 4], F32))
        gl1 = _stk.enter_context(nc.sbuf_tensor("gl1", [128, 1], F32))
        gl2 = _stk.enter_context(nc.sbuf_tensor("gl2", [128, 1], F32))
        jkv = _stk.enter_context(nc.sbuf_tensor("jkv", [128, 1], F32))
        ps = _stk.enter_context(nc.psum_tensor("ps", [128, 4096], F32))
        _sW = _stk.enter_context(nc.semaphore("sW"))
        _sX = _stk.enter_context(nc.semaphore("sX"))
        _sO = _stk.enter_context(nc.semaphore("sO"))
        _sT = _stk.enter_context(nc.semaphore("sT"))
        _sV = _stk.enter_context(nc.semaphore("sV"))
        _sS = _stk.enter_context(nc.semaphore("sS"))
        block = _stk.enter_context(nc.Block())
        sW, sX, sO = Sem(_sW), Sem(_sX), Sem(_sO)
        sT, sV, sS = Sem(_sT), Sem(_sV), Sem(_sS)
        P = Prog()
        rbc = qsb[:, 0:512]
        mrbc = qsb[:, 512:1024]
        glogsb = mrbc[0:4, :]
        gbbsb = rbc[0:32, :]
        xntok = attn[:, 0:4096]
        ysq = rt[:].bitcast(BF16)
        _attn_f32 = attn[:].bitcast(F32)
        escr = _attn_f32
        _t1b = t1[:].bitcast(BF16)
        embc_sb = _t1b[:, 0:512]
        emTrow = _t1b[0:1, 512:1024]
        rowt = _attn_f32[0:1, 0:512]
        rowa = _attn_f32[0:1, 512:1024]
        rowb = _attn_f32[0:1, 1024:1536]
        rowr = attn[0:1, 3072:3584]
        rowmr = attn[0:1, 3584:4096]

        B = lambda i: ps[:, i * 512:(i + 1) * 512]
        BN_f32 = BN[:].bitcast(F32)

        NSLOT = 3
        slot_guard = [(sT, 0)] * NSLOT
        slot_ready = [0] * NSLOT
        slot_i = [0]

        def stream(src_ap):
            i = slot_i[0] % NSLOT
            slot_i[0] += 1
            g = slot_guard[i]
            P.op("gpsimd", [g], lambda e, s=src_ap, i=i: e.dma_start(
                wpool[:, i * 2048:(i + 1) * 2048], s), inc=sW, inc_by=16)
            slot_ready[i] = sW.n
            return i

        def slot_ap(i):
            return wpool[:, i * 2048:(i + 1) * 2048]

        for dst, src in [(iden, iden_d),
                         (maskc, maskc_d), (maskr, maskr_d), (e4, e4_d),
                         (ones1, ones1_d), (onesp, onesp_d)]:
            P.op("sync", [], lambda e, d=dst, s=src: e.dma_start(d[:], s), inc=sX, inc_by=16)
        P.op("gpsimd", [], lambda e: e.dma_start(asb[:], ab_d), inc=sW, inc_by=16)
        P.op("gpsimd", [], lambda e: e.dma_start(bbsb[:], bb_d), inc=sW, inc_by=16)
        init_w = sW.n
        init_x = sX.n
        P.op("vector", [], lambda e: e.memset(Ppsb[32:33, :], 1.0), inc=sV)

        def ln_tok(src_ap, first_waits, norm_waits):
            P.op("vector", list(first_waits),
                 lambda e, a=src_ap: e.reduce_sum(st2[:, 0:1], a, axis=AX.X), inc=sV)
            v0_ = sV.n
            P.op("scalar", list(first_waits),
                 lambda e, a=src_ap: e.activation(BC[:, 0:4096], a, AF.Square), inc=sS)
            s0_ = sS.n
            P.op("vector", [(sS, s0_)],
                 lambda e: e.reduce_sum(st2[:, 1:2], BC[:, 0:4096], axis=AX.X))
            P.op("vector", [(sV, v0_)],
                 lambda e: e.tensor_scalar_mul(gl2[:], st2[:, 0:1], 1.0 / 4096.0))
            P.op("vector", [], lambda e: e.tensor_scalar_mul(lnt[:], st2[:, 1:2], 1.0 / 4096.0))
            P.op("vector", [], lambda e: e.tensor_tensor(gl1[:], gl2[:], gl2[:], ALU.mult))
            P.op("vector", [], lambda e: e.tensor_sub(lnt[:], lnt[:], gl1[:]))
            P.op("vector", [], lambda e: e.tensor_scalar_add(lnt[:], lnt[:], EPS), inc=sV)
            v1_ = sV.n
            P.op("scalar", [(sV, v1_)], lambda e: e.activation(lnt[:], lnt[:], AF.Ln))
            P.op("scalar", [], lambda e: e.activation(rstd[:], lnt[:], AF.Exp, scale=-0.5),
                 inc=sS)
            s1_ = sS.n
            P.op("vector", [(sS, s1_)], lambda e: e.scalar_tensor_tensor(
                negmr[:], gl2[:], -1.0, rstd[:], ALU.mult, ALU.mult), inc=sV)
            v2_ = sV.n
            P.op("scalar", [(sV, v2_)] + norm_waits,
                 lambda e, a=src_ap: e.activation(
                     xntok[:], a, AF.Identity, bias=negmr[:], scale=rstd[:]), inc=sS)
            return sS.n

        tr_evac = {0: 0, 1: 0}

        def transpose_tile_to_fm(dst_buf, tt, src_ready_s, extra_first=()):
            for fg in range(8):
                bank = fg % 2
                w = list(extra_first) if fg == 0 else []
                if fg == 0:
                    w.append((sS, src_ready_s))
                if tr_evac[bank]:
                    w.append((sV, tr_evac[bank]))
                for k in range(4):
                    f = fg * 4 + k
                    P.op("tensor", w if k == 0 else [],
                         lambda e, f=f, k=k, bank=bank: e.matmul(
                             B(bank)[:, k * 128:(k + 1) * 128],
                             xntok[:, f * 128:(f + 1) * 128], iden[:],
                             start=True, stop=True),
                         inc=sT if k == 3 else None)
                t_done = sT.n
                dst = dst_buf[:].rearrange("p (c t) -> p c t", t=SL)[
                    :, 4 * fg:4 * fg + 4, tt * 128:(tt + 1) * 128]
                P.op("vector", [(sT, t_done)],
                     lambda e, d=dst, bank=bank: e.tensor_copy(
                         d, B(bank)[:].rearrange("p (c t) -> p c t", t=128)), inc=sV)
                tr_evac[bank] = sV.n
            return sT.n

        def rope(dst_ap, bank, tcol, t_ready, war=()):
            P.op("vector", [(sT, t_ready)] + list(war),
                 lambda e: e.stream_shuffle(rt[0:64, :], B(bank)[64:128, :], IDMASK))
            P.op("vector", [], lambda e: e.stream_shuffle(rt[64:128, :], B(bank)[0:64, :],
                                                          IDMASK))
            P.op("vector", [], lambda e: e.tensor_tensor(
                t1[:], B(bank)[:], tabc[:, 0:512], ALU.mult))
            P.op("vector", [], lambda e: e.tensor_tensor(
                rt[:], rt[:], tabsg[:, 0:512], ALU.mult))
            P.op("vector", [], lambda e, d=dst_ap: e.tensor_tensor(d, t1[:], rt[:], ALU.add),
                 inc=sV)
            return sV.n

        def chain32(bank, unit_dram, rhs_buf, first_waits, last_inc=True, pre=None):
            if pre is not None:
                i1, i2 = pre
            else:
                i1 = stream(unit_dram[:, 0:2048])
                i2 = stream(unit_dram[:, 2048:4096])
            for c in range(32):
                ii, off = (i1, c) if c < 16 else (i2, c - 16)
                w = []
                if c == 0:
                    w = list(first_waits) + [(sW, slot_ready[i1])]
                elif c == 16:
                    w = [(sW, slot_ready[i2])]
                P.op("tensor", w, lambda e, ii=ii, off=off, c=c, bank=bank: e.matmul(
                    B(bank), slot_ap(ii)[:, off * 128:(off + 1) * 128],
                    rhs_buf[:, c * SL:c * SL + 512],
                    start=(c == 0), stop=(c == 31 and last_inc)),
                    inc=sT if (c == 31 and last_inc) else None)
                if c == 15:
                    slot_guard[i1] = (sT, sT.n + 1)
                if c == 31:
                    slot_guard[i2] = (sT, sT.n if last_inc else sT.n + 1)
            return sT.n

        prev_slice_v = 0
        prev_out_o = 0
        ctx_evac_v = 0
        psT_guard = [0, 0]
        attnT_guard4 = [0, 0, 0]
        evacmult_v = [0]
        emT_evac_v = [0]
        qread_slot = [0, 0]
        s_emf_prev = [0]
        last_rope_v = [0]

        for s in range(NS):
            t0 = s * SL
            nkc = 4 * (s + 1)

            prev_tr_t = 0
            for tt in range(NT):
                g = s * NT + tt
                P.op("gpsimd", [(sV, prev_slice_v)] if tt == 0 else [],
                     lambda e, g=g, tt=tt: e.dma_start(
                         BX[:, tt * E:(tt + 1) * E], x_d[g * 128:(g + 1) * 128, :]),
                     inc=sW, inc_by=16)
                xw = sW.n
                s_xn = ln_tok(BX[:, tt * E:(tt + 1) * E], [(sW, xw)],
                              [(sT, prev_tr_t)] if prev_tr_t else [])
                prev_tr_t = transpose_tile_to_fm(
                    BN, tt, s_xn, extra_first=[(sO, prev_out_o)] if tt == 0 else [])
            xn_ready = sV.n

            P.op("sync", [(sV, last_rope_v[0])] if last_rope_v[0] else [],
                 lambda e, t0=t0: e.dma_start(tabc[:], tabc_d[:, t0:t0 + 512]),
                 inc=sX, inc_by=16)
            P.op("sync", [], lambda e, t0=t0: e.dma_start(
                tabsg[:], tabs_d[:, t0:t0 + 512]), inc=sX, inc_by=16)
            tab_x = sX.n
            kv_rope = {}
            for d in range(NKV):
                bank = 2 + d % 2
                fw = [(sV, xn_ready)] if d == 0 else []
                if d >= 2:
                    fw.append((sV, kv_rope[d - 2]))
                t_k = chain32(bank, wk_d[d], BN, fw)
                kv_rope[d] = rope(Ksb[:, d * T + t0:d * T + t0 + 512], bank, t0, t_k,
                                  war=[(sX, tab_x)] if d == 0 else [])
            q0_pre = (stream(wq_d[0][:, 0:2048]), stream(wq_d[0][:, 2048:4096]))
            vslot_guard = [(sT, 0), (sT, 0)]
            for c in range(32):
                vi = c % 2
                P.op("gpsimd", [vslot_guard[vi]],
                     lambda e, c=c, vi=vi: e.dma_start(
                         vslot[:, vi * 1024:(vi + 1) * 1024], wv_d[c]),
                     inc=sW, inc_by=16)
                v_w = sW.n
                for tb in range(NT):
                    for half in range(2):
                        w = []
                        if tb == 0 and half == 0:
                            w.append((sW, v_w))
                            if c == 0:
                                w.append((sV, xn_ready))
                        if c == 0 and tb == 1 and half == 0:
                            w += [(sV, kv_rope[6]), (sV, kv_rope[7])]
                        P.op("tensor", w,
                             lambda e, c=c, tb=tb, vi=vi, half=half: e.matmul(
                                 ps[:, tb * 1024 + half * 512:
                                    tb * 1024 + (half + 1) * 512],
                                 BN[:, c * SL + tb * 128:c * SL + (tb + 1) * 128],
                                 vslot[:, vi * 1024 + half * 512:
                                       vi * 1024 + (half + 1) * 512],
                                 start=(c == 0), stop=(c == 31),
                                 skip_group_check=True),
                             inc=sT if (tb == NT - 1 and half == 1) else None)
                vslot_guard[vi] = (sT, sT.n)
            t_vmm = sT.n
            for tb in range(NT):
                dst = Vsb[:, (s * NT + tb) * 1024:(s * NT + tb + 1) * 1024]
                if tb < 2:
                    P.op("scalar", [(sT, t_vmm)] if tb == 0 else [],
                         lambda e, dst=dst, tb=tb: e.activation(
                             dst, ps[:, tb * 1024:(tb + 1) * 1024], AF.Identity),
                         inc=sS)
                else:
                    P.op("vector", [(sT, t_vmm)] if tb == 2 else [],
                         lambda e, dst=dst, tb=tb: e.tensor_copy(
                             dst, ps[:, tb * 1024:(tb + 1) * 1024]), inc=sV)
            vevac_s = sS.n
            vevac_v = sV.n

            if phases < 3:
                continue
            exp_tok = {}
            grp = [list(range((nkc * j) // 4, (nkc * (j + 1)) // 4))
                   for j in range(4)]

            q_t = chain32(7, wq_d[0], BN, [(sV, vevac_v), (sS, vevac_s)],
                          pre=q0_pre)
            rope_q = {0: rope(qsb[:, 0:512], 7, t0, q_t,
                              war=[(sT, qread_slot[0])] if qread_slot[0] else [])}
            for h in range(NH):
                kv = h // GQ
                last_stc = 0
                ctx_last = 0
                for j in range(4):
                    valid = t0 + (j + 1) * 128
                    nch = (valid + 511) // 512
                    if j == 0:
                        w = [(sV, rope_q[h]), (sV, kv_rope[kv])]
                        if s_emf_prev[0]:
                            w.append((sS, s_emf_prev[0]))
                        if emT_evac_v[0]:
                            w.append((sV, emT_evac_v[0]))
                        if h == 0:
                            w += [(sV, vevac_v), (sS, vevac_s)]
                    else:
                        w = [(sS, exp_tok[(h, j - 1)])]
                    for kc in range(nch):
                        k1 = min(valid, (kc + 1) * 512)
                        P.op("tensor", w if kc == 0 else [],
                             lambda e, kc=kc, k1=k1, kv=kv, j=j, h=h: e.matmul(
                                 ps[:, kc * 512:k1],
                                 qsb[:, (h % 2) * 512 + j * 128:
                                     (h % 2) * 512 + (j + 1) * 128],
                                 Ksb[:, kv * T + kc * 512:kv * T + k1],
                                 start=True, stop=True))
                    P.op("tensor", [], lambda e, valid=valid: e.matmul(
                        ps[:, valid - 128:valid], iden[:], maskc[:],
                        start=False, stop=True, skip_group_check=True), inc=sT)
                    t_sc = sT.n
                    P.op("vector", [(sT, t_sc)], lambda e, valid=valid, j=j:
                         e.reduce_max(nmx[:, j:j + 1], ps[:, 0:valid], axis=AX.X))
                    P.op("vector", [], lambda e: e.tensor_copy(jkv[:], rstd[:]))
                    P.op("vector", [], lambda e, j=j: e.tensor_scalar_mul(
                        nmx[:, j:j + 1], nmx[:, j:j + 1], -INVSQ), inc=sV)
                    v_mx = sV.n
                    P.op("scalar", [(sV, v_mx)],
                         lambda e, valid=valid, j=j: e.activation(
                             escr[:, 0:valid], ps[:, 0:valid], AF.Exp,
                             bias=nmx[:, j:j + 1], scale=INVSQ,
                             accum_out=lsum[:, j:j + 1]), inc=sS)
                    exp_tok[(h, j)] = sS.n
                    for c in grp[j]:
                        d = c - 4 * s
                        coloff = max(0, d) * 128
                        pb = 4 + (c & 1)
                        diag = d >= 0
                        wc = []
                        if psT_guard[c & 1]:
                            wc.append((sS, psT_guard[c & 1]))
                        if c == 0 and evacmult_v[0]:
                            wc.append((sV, evacmult_v[0]))
                        if h == 0 and c <= 1:
                            wc.append((sV, vevac_v))
                        P.op("tensor", wc,
                             lambda e, pb=pb, coloff=coloff, c=c, kv=kv, h=h,
                             diag=diag: e.matmul(
                                 ps[:, pb * 512 + coloff:(pb + 1) * 512],
                                 Ksb[:, kv * T + c * 128:kv * T + (c + 1) * 128],
                                 qsb[:, (h % 2) * 512 + coloff:(h % 2) * 512 + 512],
                                 start=True, stop=not diag, skip_group_check=True),
                             inc=None if diag else sT)
                        if diag:
                            P.op("tensor", [],
                                 lambda e, pb=pb, coloff=coloff: e.matmul(
                                     ps[:, pb * 512 + coloff:pb * 512 + coloff + 128],
                                     iden[:], maskr[:], start=False, stop=True,
                                     skip_group_check=True), inc=sT)
                        t_stc = sT.n
                        last_stc = t_stc
                        slot = c % 3
                        we = [(sT, t_stc)]
                        if attnT_guard4[slot]:
                            we.append((sT, attnT_guard4[slot]))
                        P.op("scalar", we,
                             lambda e, slot=slot, coloff=coloff, pb=pb: e.activation(
                                 attnT[:, slot * 512 + coloff:(slot + 1) * 512],
                                 ps[:, pb * 512 + coloff:(pb + 1) * 512], AF.Exp,
                                 scale=INVSQ), inc=sS)
                        exp_c = sS.n
                        psT_guard[c & 1] = exp_c
                        wx = [(sS, exp_c)]
                        if c == 0:
                            if evacmult_v[0]:
                                wx.append((sV, evacmult_v[0]))
                            if h == 0:
                                wx.append((sV, vevac_v))
                        P.op("tensor", wx,
                             lambda e, c=c, kv=kv, slot=slot, coloff=coloff,
                             nkc=nkc: e.matmul(
                                 B(6)[:, coloff:512],
                                 Vsb[:, c * 1024 + kv * 128:c * 1024 + (kv + 1) * 128],
                                 attnT[:, slot * 512 + coloff:(slot + 1) * 512],
                                 start=(c == 0), stop=(c == nkc - 1),
                                 skip_group_check=True), inc=sT)
                        attnT_guard4[slot] = sT.n
                        ctx_last = sT.n
                qread_slot[h % 2] = last_stc
                P.op("vector", [(sS, exp_tok[(h, 3)])], lambda e: e.reciprocal(
                    rlsum[:, 0:4], lsum[:, 0:4]))
                P.op("vector", [], lambda e: e.tensor_copy(jkv[:], rstd[:]))
                P.op("scalar", [], lambda e: e.activation(
                    emf[:, 0:4], nmx[:, 0:4], AF.Exp), inc=sS)
                s_emf_prev[0] = sS.n
                P.op("vector", [(sS, s_emf_prev[0])], lambda e: e.tensor_tensor(
                    emf[:, 0:4], emf[:, 0:4], rlsum[:, 0:4], ALU.mult), inc=sV)
                v_emf = sV.n
                for jj in range(4):
                    wb = [(sV, v_emf), (sS, exp_tok[(h, 3)])] if jj == 0 else []
                    P.op("tensor", wb, lambda e, jj=jj: e.matmul(
                        ps[0:1, jj * 128:(jj + 1) * 128], emf[:, jj:jj + 1],
                        iden[:], start=True, stop=True),
                        inc=sT if jj == 3 else None)
                t_emt = sT.n
                P.op("vector", [(sT, t_emt)], lambda e: e.tensor_copy(
                    emTrow, ps[0:1, 0:512]), inc=sV)
                emT_evac_v[0] = sV.n
                wb = [(sV, emT_evac_v[0])]
                if psT_guard[0]:
                    wb.append((sS, psT_guard[0]))
                P.op("tensor", wb, lambda e: e.matmul(
                    ps[:, 4 * 512:5 * 512], ones1[:], emTrow,
                    start=True, stop=True), inc=sT)
                t_bc = sT.n
                psT_guard[0] = 0
                if h + 1 < NH:
                    q_t = chain32(7, wq_d[h + 1], BN, [(sV, rope_q[h])])
                P.op("vector", [(sT, t_bc)], lambda e: e.tensor_copy(
                    embc_sb[:], ps[:, 4 * 512:5 * 512]))
                P.op("vector", [(sT, ctx_last)], lambda e, h=h: e.tensor_tensor(
                    BC[:, h * 512:(h + 1) * 512], B(6), embc_sb[:], ALU.mult),
                    inc=sV)
                evacmult_v[0] = sV.n
                if h + 1 < NH:
                    rope_q[h + 1] = rope(
                        qsb[:, ((h + 1) % 2) * 512:((h + 1) % 2) * 512 + 512],
                        7, t0, q_t,
                        war=[(sT, qread_slot[(h + 1) % 2])]
                        if qread_slot[(h + 1) % 2] else [])
            ctx_evac_v = evacmult_v[0]
            last_rope_v[0] = rope_q[NH - 1]

            pp_ready = [0]

            def emit_gates(xn2_v):
                for c in range(32):
                    w = [(sV, xn2_v), (sW, init_w)] if c == 0 else []
                    P.op("tensor", w, lambda e, c=c: e.matmul(
                        ps[0:32, 7 * 512:8 * 512], asb[:, c * 36:c * 36 + 32],
                        BN[:, c * SL:c * SL + 512], start=(c == 0), stop=(c == 31)))
                    P.op("tensor", [], lambda e, c=c: e.matmul(
                        ps[0:4, 3 * 512:4 * 512], asb[:, c * 36 + 32:c * 36 + 36],
                        BN[:, c * SL:c * SL + 512], start=(c == 0), stop=(c == 31)),
                        inc=sT if c == 31 else None)
                t_pg = sT.n
                P.op("vector", [(sT, t_pg)], lambda e: e.tensor_copy(
                    glogsb[:], ps[0:4, 3 * 512:4 * 512]), inc=sV)
                v_gl = sV.n
                for k in range(4):
                    P.op("tensor", [(sV, v_gl), (sX, init_x)] if k == 0 else [],
                         lambda e, k=k: e.matmul(
                             B(5)[:, k * 4:(k + 1) * 4],
                             glogsb[:, k * 128:(k + 1) * 128], iden[0:4, 0:4],
                             start=True, stop=True), inc=sT if k == 3 else None)
                t_gt = sT.n
                for k in range(4):
                    P.op("vector", [(sT, t_gt)] if k == 0 else [],
                         lambda e, k=k: e.reduce_max(
                             gl1[:], B(5)[:, k * 4:(k + 1) * 4], axis=AX.X, negate=True),
                         inc=sV)
                    v_m = sV.n
                    P.op("scalar", [(sV, v_m)], lambda e, k=k: e.activation(
                        gtok[:, k * 4:(k + 1) * 4], B(5)[:, k * 4:(k + 1) * 4], AF.Exp,
                        bias=gl1[:]), inc=sS)
                    s_e = sS.n
                    P.op("vector", [(sS, s_e)], lambda e, k=k: e.reduce_sum(
                        gl2[:], gtok[:, k * 4:(k + 1) * 4], axis=AX.X))
                    P.op("vector", [], lambda e: e.tensor_copy(jkv[:], rstd[:]))
                    P.op("vector", [], lambda e: e.reciprocal(gl1[:], gl2[:]))
                    P.op("vector", [], lambda e: e.tensor_copy(jkv[:], rstd[:]))
                    P.op("vector", [], lambda e, k=k: e.tensor_scalar_mul(
                        gtok[:, k * 4:(k + 1) * 4], gtok[:, k * 4:(k + 1) * 4], gl1[:]),
                        inc=sV)
                v_gn = sV.n
                for k in range(4):
                    P.op("tensor", [(sV, v_gn)] if k == 0 else [],
                         lambda e, k=k: e.matmul(
                             ps[0:4, 2 * 512 + k * 128:2 * 512 + (k + 1) * 128],
                             gtok[:, k * 4:(k + 1) * 4], iden[:],
                             start=True, stop=True), inc=sT if k == 3 else None)
                t_gb = sT.n
                P.op("vector", [(sT, t_gb)], lambda e: e.tensor_scalar_mul(
                    glogsb[:], ps[0:4, 2 * 512:3 * 512], LORA_SCALE), inc=sV)
                v_g2 = sV.n
                P.op("tensor", [(sV, v_g2)], lambda e: e.matmul(
                    ps[0:32, 2 * 512:3 * 512], e4[:], glogsb[:], start=True, stop=True),
                    inc=sT)
                t_gbb = sT.n
                P.op("vector", [(sT, t_gbb)], lambda e: e.tensor_copy(
                    gbbsb[:], ps[0:32, 2 * 512:3 * 512]), inc=sV)
                v_gbb = sV.n
                P.op("vector", [(sV, v_gbb)], lambda e: e.tensor_tensor(
                    Ppsb[0:32, :], ps[0:32, 7 * 512:8 * 512], gbbsb[:], ALU.mult), inc=sV)
                pp_ready[0] = sV.n

            out_wait = {}

            def proj_ln_residual(w_dram, src_buf, y_buf, is_ffn, first_v):
                nonlocal prev_out_o
                yev = {0: 0, 1: 0}
                pend_stats = []

                def emit_stats(oc, v_ysq):
                    P.op("tensor", [(sV, v_ysq)], lambda e, oc=oc: e.matmul(
                        B(2)[0:1, :], onesp[:], y_buf[:, oc * 512:(oc + 1) * 512],
                        start=(oc == 0), stop=(oc == 31)))
                    P.op("tensor", [], lambda e, oc=oc: e.matmul(
                        B(3)[0:1, :], onesp[:], ysq[:, (oc % 2) * 512:(oc % 2) * 512 + 512],
                        start=(oc == 0), stop=(oc == 31)), inc=sT)

                for oc in range(32):
                    bank = oc % 2
                    fw = []
                    if oc == 0:
                        fw.append((sV, first_v))
                    if yev[bank]:
                        fw.append((sS, yev[bank]))
                    t_y = chain32(bank, w_dram[oc], src_buf, fw, last_inc=not is_ffn)
                    if is_ffn:
                        P.op("tensor", [(sV, pp_ready[0])] if oc == 0 else [],
                             lambda e, oc=oc, bank=bank: e.matmul(
                                 B(bank), bbsb[:, oc * 128:(oc + 1) * 128], Ppsb[:],
                                 start=False, stop=True, skip_group_check=True), inc=sT)
                        t_y = sT.n
                    P.op("scalar", [(sT, t_y)], lambda e, oc=oc, bank=bank: e.activation(
                        y_buf[:, oc * 512:(oc + 1) * 512], B(bank), AF.Identity), inc=sS)
                    yev[bank] = sS.n
                    wq_ = [(sS, yev[bank])]
                    P.op("vector", wq_, lambda e, oc=oc: e.tensor_tensor(
                        ysq[:, (oc % 2) * 512:(oc % 2) * 512 + 512],
                        y_buf[:, oc * 512:(oc + 1) * 512],
                        y_buf[:, oc * 512:(oc + 1) * 512], ALU.mult), inc=sV)
                    pend_stats.append((oc, sV.n))
                    if len(pend_stats) >= 2:
                        emit_stats(*pend_stats.pop(0))
                while pend_stats:
                    emit_stats(*pend_stats.pop(0))
                stat_t = sT.n
                P.op("vector", [(sT, stat_t)], lambda e: e.tensor_scalar_mul(
                    rowa, B(2)[0:1, :], 1.0 / 4096.0))
                P.op("vector", [], lambda e: e.tensor_scalar_mul(
                    rowb, B(3)[0:1, :], 1.0 / 4096.0))
                P.op("vector", [], lambda e: e.tensor_tensor(
                    rowt, rowa, rowa, ALU.mult))
                P.op("vector", [], lambda e: e.tensor_sub(rowb, rowb, rowt))
                P.op("vector", [], lambda e: e.tensor_scalar_add(rowb, rowb, EPS),
                     inc=sV)
                v_var = sV.n
                P.op("scalar", [(sV, v_var)], lambda e: e.activation(rowb, rowb, AF.Ln))
                P.op("scalar", [], lambda e: e.activation(rowr, rowb, AF.Exp,
                                                          scale=-0.5), inc=sS)
                s_r = sS.n
                P.op("vector", [(sS, s_r)], lambda e: e.tensor_tensor(
                    rowmr, rowa, rowr, ALU.mult), inc=sV)
                v_mr = sV.n
                P.op("tensor", [(sV, v_mr)], lambda e: e.matmul(
                    B(4), ones1[:], rowr, start=True, stop=True))
                P.op("tensor", [], lambda e: e.matmul(
                    B(5), ones1[:], rowmr, start=True, stop=True), inc=sT)
                t_bc = sT.n
                P.op("vector", [(sT, t_bc)], lambda e: e.tensor_copy(rbc[:], B(4)))
                P.op("vector", [], lambda e: e.tensor_copy(mrbc[:], B(5)), inc=sV)
                v_bc = sV.n
                ln_done = {}
                for ocg in range(8):
                    for k in range(4):
                        oc = ocg * 4 + k
                        P.op("vector", [(sV, v_bc)] if oc == 0 else [],
                             lambda e, oc=oc: e.tensor_tensor(
                                 y_buf[:, oc * 512:(oc + 1) * 512],
                                 y_buf[:, oc * 512:(oc + 1) * 512], rbc[:], ALU.mult))
                        P.op("vector", [], lambda e, oc=oc: e.tensor_tensor(
                            y_buf[:, oc * 512:(oc + 1) * 512],
                            y_buf[:, oc * 512:(oc + 1) * 512], mrbc[:], ALU.subtract),
                            inc=sV if k == 3 else None)
                    ln_done[ocg] = sV.n
                ep_ev = {6: 0, 7: 0}
                for tt in range(NT):
                    for ocg in range(8):
                        bank = 6 + (ocg % 2)
                        w = [(sV, ln_done[ocg])] if tt == 0 else []
                        if ep_ev[bank]:
                            w.append((sV, ep_ev[bank]))
                        for k in range(4):
                            oc = ocg * 4 + k
                            P.op("tensor", w if k == 0 else [],
                                 lambda e, oc=oc, k=k, tt=tt, bank=bank: e.matmul(
                                     B(bank)[:, k * 128:(k + 1) * 128],
                                     y_buf[:, oc * 512 + tt * 128:oc * 512 + tt * 128 + 128],
                                     iden[:], start=True, stop=True),
                                 inc=sT if k == 3 else None)
                        t_ep = sT.n
                        if not is_ffn:
                            dst = BX[:, tt * E + ocg * 512:tt * E + (ocg + 1) * 512]
                            P.op("vector", [(sT, t_ep)], lambda e, d=dst, bank=bank:
                                 e.tensor_tensor(d, B(bank), d, ALU.add), inc=sV)
                        else:
                            stg = BN_f32[:, (tt % 2) * 4096 + ocg * 512:
                                         (tt % 2) * 4096 + (ocg + 1) * 512]
                            xr = BX[:, tt * E + ocg * 512:tt * E + (ocg + 1) * 512]
                            w2 = [(sT, t_ep)]
                            if tt >= 2 and ocg == 0:
                                w2.append((sO, out_wait[tt - 2]))
                            P.op("vector", w2, lambda e, stg=stg, xr=xr, bank=bank:
                                 e.tensor_tensor(stg, B(bank), xr, ALU.add), inc=sV)
                        ep_ev[bank] = sV.n
                    if is_ffn:
                        v_tt = sV.n
                        g = s * NT + tt
                        P.op("sync", [(sV, v_tt)], lambda e, g=g, tt=tt: e.dma_start(
                            out_d[g * 128:(g + 1) * 128, :],
                            BN_f32[:, (tt % 2) * 4096:(tt % 2) * 4096 + 4096]),
                            inc=sO, inc_by=16)
                        out_wait[tt] = sO.n
                if is_ffn:
                    prev_out_o = sO.n

            if phases < 4:
                continue
            proj_ln_residual(wo_d, BC, BN, is_ffn=False, first_v=ctx_evac_v)
            x_prime_v = sV.n

            if phases < 5:
                continue
            prev_tr_t = 0
            for tt in range(NT):
                s_xn = ln_tok(BX[:, tt * E:(tt + 1) * E],
                              [(sV, x_prime_v)] if tt == 0 else [],
                              [(sT, prev_tr_t)] if prev_tr_t else [])
                prev_tr_t = transpose_tile_to_fm(BN, tt, s_xn)
            xn2_ready = sV.n

            if phases < 6:
                continue
            emit_gates(xn2_ready)
            proj_ln_residual(wf_d, BN, BC, is_ffn=True, first_v=pp_ready[0])
            prev_slice_v = sV.n

        if phases < 6:
            P.op("sync", [(sV, sV.n), (sS, sS.n)],
                 lambda e: e.dma_start(out_d[0:128, :], BX[:].bitcast(F32)[:, 0:4096]),
                 inc=sO, inc_by=16)
        final_o = sO.n

        @block.sync
        def _(e):
            P.replay("sync", e)
            e.wait_ge(_sO, final_o)

        @block.gpsimd
        def _(e):
            P.replay("gpsimd", e)

        @block.tensor
        def _(e):
            P.replay("tensor", e)

        @block.vector
        def _(e):
            P.replay("vector", e)

        @block.scalar
        def _(e):
            P.replay("scalar", e)

    return nc



def _rope_tables(T):
    inv_freq = 1.0 / (500000.0 ** (np.arange(0, 128, 2, dtype=np.float32) / 128.0))
    wavelen = 2.0 * math.pi / inv_freq
    cut_lo = 8192.0 / 4.0
    cut_hi = 8192.0 / 1.0
    inv_freq = np.where(wavelen > cut_hi, inv_freq / 32.0, inv_freq).astype(np.float32)
    smooth = ((8192.0 / wavelen - 1.0) / 3.0).astype(np.float32)
    smoothed = ((1.0 - smooth) * inv_freq / 32.0 + smooth * inv_freq).astype(np.float32)
    is_medium = (~(wavelen < cut_hi)) & (~(wavelen > cut_lo))
    inv_freq = np.where(is_medium, smoothed, inv_freq).astype(np.float32)
    pos = np.arange(T, dtype=np.float32)
    freqs = pos[:, None] * inv_freq[None, :]
    cos = np.cos(freqs).astype(np.float32).T
    sin = np.sin(freqs).astype(np.float32).T
    tabc = np.concatenate([cos, cos], 0).astype(np.float16)
    tabsg = np.concatenate([-sin, sin], 0).astype(np.float16)
    return tabc, tabsg


def _retile(WT):
    I, O = WT.shape
    return np.ascontiguousarray(
        WT.reshape(I // 128, 128, O // 128, 128).transpose(2, 1, 0, 3).reshape(
            O // 128, 128, I))


def prep_inputs(x, Wq, Wk, Wv, Wo, W, b, A, Bm, Wr, T):
    wq = _retile(np.ascontiguousarray(Wq.T))
    wk = _retile(np.ascontiguousarray(Wk.T))
    wv = np.ascontiguousarray(Wv.T).reshape(32, 128, 1024)
    wo = _retile(np.ascontiguousarray(Wo.T))
    wf = _retile(np.ascontiguousarray(W.T))
    A2 = A.reshape(32, 4096)
    AW = np.concatenate([A2, Wr], 0).T
    ab = np.ascontiguousarray(AW.reshape(32, 128, 36).transpose(1, 0, 2).reshape(128, 1152))
    bb = np.ascontiguousarray(
        np.concatenate([Bm.transpose(0, 2, 1).reshape(32, 4096), b[None, :]], 0))
    tabc, tabsg = _rope_tables(T)
    iden = np.eye(128, dtype=np.float32).astype(ml_dtypes.bfloat16)
    maskc = np.triu(np.full((128, 128), -1e38, dtype=np.float32), k=1).astype(
        ml_dtypes.bfloat16)
    maskr = np.tril(np.full((128, 128), -1e38, dtype=np.float32), k=-1).astype(
        ml_dtypes.bfloat16)
    e4 = np.zeros((4, 32), dtype=np.float32)
    for i in range(4):
        e4[i, i * 8:(i + 1) * 8] = 1.0
    shared = dict(
        wq=np.ascontiguousarray(wq, np.float32), wk=np.ascontiguousarray(wk, np.float32),
        wv=np.ascontiguousarray(wv, np.float32), wo=np.ascontiguousarray(wo, np.float32),
        wf=np.ascontiguousarray(wf, np.float32), ab=np.ascontiguousarray(ab, np.float32),
        bb=np.ascontiguousarray(bb, np.float32), tabc=tabc, tabs=tabsg,
        iden=iden, maskc=maskc, maskr=maskr, e4=e4.astype(ml_dtypes.bfloat16),
        ones1=np.ones((1, 128), np.float32).astype(ml_dtypes.bfloat16),
        onesp=np.ones((128, 1), np.float32).astype(ml_dtypes.bfloat16))
    in_maps = []
    for bi in range(x.shape[0]):
        m = dict(shared)
        m["x"] = np.ascontiguousarray(x[bi], np.float32)
        in_maps.append(m)
    return in_maps


def kernel(L0_attn_x, L0_attn_Wq, L0_attn_Wk, L0_attn_Wv, L0_attn_Wo,
           L0_ffn_W, L0_ffn_b, L0_ffn_A, L0_ffn_Bm, L0_ffn_Wr):
    x = np.asarray(L0_attn_x, np.float32)
    Bn, T, _ = x.shape
    in_maps = prep_inputs(
        x, np.asarray(L0_attn_Wq, np.float32), np.asarray(L0_attn_Wk, np.float32),
        np.asarray(L0_attn_Wv, np.float32), np.asarray(L0_attn_Wo, np.float32),
        np.asarray(L0_ffn_W, np.float32), np.asarray(L0_ffn_b, np.float32),
        np.asarray(L0_ffn_A, np.float32), np.asarray(L0_ffn_Bm, np.float32),
        np.asarray(L0_ffn_Wr, np.float32), T)
    nc = build(T=T)
    res = run_bass_kernel_spmd(nc, in_maps, core_ids=list(range(Bn)))
    return np.stack([r["out"] for r in res.results], axis=0).astype(np.float32)


# version 06
# speedup vs baseline: 1.3455x; 1.1163x over previous
import math
import numpy as np
import ml_dtypes
import concourse.bass as bass
import concourse.mybir as mybir
from concourse.bass_utils import run_bass_kernel_spmd

F32 = mybir.dt.float32
F16 = mybir.dt.float16
BF16 = mybir.dt.bfloat16
AF = mybir.ActivationFunctionType
ALU = mybir.AluOpType
AX = mybir.AxisListType

E = 4096
NH = 32
NKV = 8
DH = 128
GQ = NH // NKV
EPS = 1e-5
INVSQ = 1.0 / math.sqrt(DH)
LORA_SCALE = 2.0
IDMASK = list(range(32))


class Sem:
    def __init__(self, h):
        self.h = h
        self.n = 0


class Prog:
    def __init__(self):
        self.ops = []

    def op(self, eng, waits, fn, inc=None, inc_by=1):
        incs = []
        if inc is not None:
            inc.n += inc_by
            incs.append((inc, inc_by))
        self.ops.append((eng, [(s, v) for (s, v) in waits if v > 0], fn, incs))

    def replay(self, eng_name, eng):
        waited = {}
        for (e, waits, fn, incs) in self.ops:
            if e != eng_name:
                continue
            for (s, v) in waits:
                if waited.get(id(s), 0) < v:
                    eng.wait_ge(s.h, v)
                    waited[id(s)] = v
            inst = fn(eng)
            for (s, k) in incs:
                inst.then_inc(s.h, k)


def build(T=2048, SL=512, dbg=False, phases=6):
    NS = T // SL
    NT = SL // 128
    nc = bass.Bass("TRN2", target_bir_lowering=False)

    x_d = nc.dram_tensor("x", [T, E], F32, kind="ExternalInput").ap()
    out_d = nc.dram_tensor("out", [T, E], F32, kind="ExternalOutput").ap()
    wq_d = nc.dram_tensor("wq", [NH, 128, E], F32, kind="ExternalInput").ap()
    wk_d = nc.dram_tensor("wk", [NKV, 128, E], F32, kind="ExternalInput").ap()
    wv_d = nc.dram_tensor("wv", [32, 128, 1024], F32, kind="ExternalInput").ap()
    wo_d = nc.dram_tensor("wo", [32, 128, E], F32, kind="ExternalInput").ap()
    wf_d = nc.dram_tensor("wf", [32, 128, E], F32, kind="ExternalInput").ap()
    ab_d = nc.dram_tensor("ab", [128, 1152], F32, kind="ExternalInput").ap()
    bb_d = nc.dram_tensor("bb", [33, E], F32, kind="ExternalInput").ap()
    tabc_d = nc.dram_tensor("tabc", [128, T], F16, kind="ExternalInput").ap()
    tabs_d = nc.dram_tensor("tabs", [128, T], F16, kind="ExternalInput").ap()
    iden_d = nc.dram_tensor("iden", [128, 128], BF16, kind="ExternalInput").ap()
    maskc_d = nc.dram_tensor("maskc", [128, 128], BF16, kind="ExternalInput").ap()
    maskr_d = nc.dram_tensor("maskr", [128, 128], BF16, kind="ExternalInput").ap()
    e4_d = nc.dram_tensor("e4", [4, 32], BF16, kind="ExternalInput").ap()
    ones1_d = nc.dram_tensor("ones1", [1, 128], BF16, kind="ExternalInput").ap()
    onesp_d = nc.dram_tensor("onesp", [128, 1], BF16, kind="ExternalInput").ap()
    import contextlib
    with contextlib.ExitStack() as _stk:
        Ksb = _stk.enter_context(nc.sbuf_tensor("Ksb", [128, NKV * T], BF16))
        Vsb = _stk.enter_context(nc.sbuf_tensor("Vsb", [128, (T // 128) * 1024], BF16))
        BX = _stk.enter_context(nc.sbuf_tensor("BX", [128, NT * E], BF16))
        BN = _stk.enter_context(nc.sbuf_tensor("BN", [128, 32 * SL], BF16))
        BC = _stk.enter_context(nc.sbuf_tensor("BC", [128, 32 * SL], BF16))
        attn = _stk.enter_context(nc.sbuf_tensor("attn", [128, 2 * 2048], BF16))
        attnT = _stk.enter_context(nc.sbuf_tensor("attnT", [128, 3 * 512], BF16))
        wpool = _stk.enter_context(nc.sbuf_tensor("wpool", [128, 3 * 2048], BF16))
        vslot = _stk.enter_context(nc.sbuf_tensor("vslot", [128, 2 * 1024], BF16))
        tabc = _stk.enter_context(nc.sbuf_tensor("tabc_s", [128, 512], F16))
        tabsg = _stk.enter_context(nc.sbuf_tensor("tabsg_s", [128, 512], F16))
        qsb = _stk.enter_context(nc.sbuf_tensor("qsb", [128, 1024], BF16))
        rt = _stk.enter_context(nc.sbuf_tensor("rt", [128, 512], F32))
        t1 = _stk.enter_context(nc.sbuf_tensor("t1", [128, 512], F32))
        iden = _stk.enter_context(nc.sbuf_tensor("iden_s", [128, 128], BF16))
        maskc = _stk.enter_context(nc.sbuf_tensor("maskc_s", [128, 128], BF16))
        maskr = _stk.enter_context(nc.sbuf_tensor("maskr_s", [128, 128], BF16))
        emf = _stk.enter_context(nc.sbuf_tensor("emf", [128, 4], BF16))
        e4 = _stk.enter_context(nc.sbuf_tensor("e4_s", [4, 32], BF16))
        ones1 = _stk.enter_context(nc.sbuf_tensor("ones1_s", [1, 128], BF16))
        onesp = _stk.enter_context(nc.sbuf_tensor("onesp_s", [128, 1], BF16))
        asb = _stk.enter_context(nc.sbuf_tensor("asb", [128, 1152], BF16))
        bbsb = _stk.enter_context(nc.sbuf_tensor("bbsb", [33, E], BF16))
        Ppsb = _stk.enter_context(nc.sbuf_tensor("Ppsb", [33, 512], BF16))
        gtok = _stk.enter_context(nc.sbuf_tensor("gtok", [128, 16], BF16))
        st2 = _stk.enter_context(nc.sbuf_tensor("st2", [128, 2], F32))
        rstd = _stk.enter_context(nc.sbuf_tensor("rstd", [128, 1], F32))
        negmr = _stk.enter_context(nc.sbuf_tensor("negmr", [128, 1], F32))
        lnt = _stk.enter_context(nc.sbuf_tensor("lnt", [128, 1], F32))
        nmx = _stk.enter_context(nc.sbuf_tensor("nmx", [128, 4], F32))
        lsum = _stk.enter_context(nc.sbuf_tensor("lsum", [128, 4], F32))
        rlsum = _stk.enter_context(nc.sbuf_tensor("rlsum", [128, 4], F32))
        gl1 = _stk.enter_context(nc.sbuf_tensor("gl1", [128, 1], F32))
        gl2 = _stk.enter_context(nc.sbuf_tensor("gl2", [128, 1], F32))
        jkv = _stk.enter_context(nc.sbuf_tensor("jkv", [128, 1], F32))
        ps = _stk.enter_context(nc.psum_tensor("ps", [128, 4096], F32))
        _sW = _stk.enter_context(nc.semaphore("sW"))
        _sX = _stk.enter_context(nc.semaphore("sX"))
        _sO = _stk.enter_context(nc.semaphore("sO"))
        _sT = _stk.enter_context(nc.semaphore("sT"))
        _sV = _stk.enter_context(nc.semaphore("sV"))
        _sS = _stk.enter_context(nc.semaphore("sS"))
        block = _stk.enter_context(nc.Block())
        sW, sX, sO = Sem(_sW), Sem(_sX), Sem(_sO)
        sT, sV, sS = Sem(_sT), Sem(_sV), Sem(_sS)
        P = Prog()
        rbc = qsb[:, 0:512]
        mrbc = qsb[:, 512:1024]
        glogsb = mrbc[0:4, :]
        gbbsb = rbc[0:32, :]
        xntok = attn[:, 0:4096]
        ysq = rt[:].bitcast(BF16)
        _attn_f32 = attn[:].bitcast(F32)
        escr = _attn_f32
        _t1b = t1[:].bitcast(BF16)
        embc_sb = _t1b[:, 0:512]
        emTrow = _t1b[0:1, 512:1024]
        rowt = _attn_f32[0:1, 0:512]
        rowa = _attn_f32[0:1, 512:1024]
        rowb = _attn_f32[0:1, 1024:1536]
        rowr = attn[0:1, 3072:3584]
        rowmr = attn[0:1, 3584:4096]

        B = lambda i: ps[:, i * 512:(i + 1) * 512]
        BN_f32 = BN[:].bitcast(F32)

        NSLOT = 3
        slot_guard = [(sT, 0)] * NSLOT
        slot_ready = [0] * NSLOT
        slot_i = [0]

        def stream(src_ap):
            i = slot_i[0] % NSLOT
            slot_i[0] += 1
            g = slot_guard[i]
            P.op("gpsimd", [g], lambda e, s=src_ap, i=i: e.dma_start(
                wpool[:, i * 2048:(i + 1) * 2048], s), inc=sW, inc_by=16)
            slot_ready[i] = sW.n
            return i

        def slot_ap(i):
            return wpool[:, i * 2048:(i + 1) * 2048]

        for dst, src in [(iden, iden_d),
                         (maskc, maskc_d), (maskr, maskr_d), (e4, e4_d),
                         (ones1, ones1_d), (onesp, onesp_d)]:
            P.op("sync", [], lambda e, d=dst, s=src: e.dma_start(d[:], s), inc=sX, inc_by=16)
        P.op("gpsimd", [], lambda e: e.dma_start(asb[:], ab_d), inc=sW, inc_by=16)
        P.op("gpsimd", [], lambda e: e.dma_start(bbsb[:], bb_d), inc=sW, inc_by=16)
        init_w = sW.n
        init_x = sX.n
        P.op("vector", [], lambda e: e.memset(Ppsb[32:33, :], 1.0), inc=sV)

        def ln_tok(src_ap, first_waits, norm_waits):
            P.op("vector", list(first_waits),
                 lambda e, a=src_ap: e.reduce_sum(st2[:, 0:1], a, axis=AX.X), inc=sV)
            v0_ = sV.n
            P.op("scalar", list(first_waits),
                 lambda e, a=src_ap: e.activation(BC[:, 0:4096], a, AF.Square), inc=sS)
            s0_ = sS.n
            P.op("vector", [(sS, s0_)],
                 lambda e: e.reduce_sum(st2[:, 1:2], BC[:, 0:4096], axis=AX.X))
            P.op("vector", [(sV, v0_)],
                 lambda e: e.tensor_scalar_mul(gl2[:], st2[:, 0:1], 1.0 / 4096.0))
            P.op("vector", [], lambda e: e.tensor_scalar_mul(lnt[:], st2[:, 1:2], 1.0 / 4096.0))
            P.op("vector", [], lambda e: e.tensor_tensor(gl1[:], gl2[:], gl2[:], ALU.mult))
            P.op("vector", [], lambda e: e.tensor_sub(lnt[:], lnt[:], gl1[:]))
            P.op("vector", [], lambda e: e.tensor_scalar_add(lnt[:], lnt[:], EPS), inc=sV)
            v1_ = sV.n
            P.op("scalar", [(sV, v1_)], lambda e: e.activation(lnt[:], lnt[:], AF.Ln))
            P.op("scalar", [], lambda e: e.activation(rstd[:], lnt[:], AF.Exp, scale=-0.5),
                 inc=sS)
            s1_ = sS.n
            P.op("vector", [(sS, s1_)], lambda e: e.scalar_tensor_tensor(
                negmr[:], gl2[:], -1.0, rstd[:], ALU.mult, ALU.mult), inc=sV)
            v2_ = sV.n
            P.op("scalar", [(sV, v2_)] + norm_waits,
                 lambda e, a=src_ap: e.activation(
                     xntok[:], a, AF.Identity, bias=negmr[:], scale=rstd[:]), inc=sS)
            return sS.n

        tr_evac = {0: 0, 1: 0}

        def transpose_tile_to_fm(dst_buf, tt, src_ready_s, extra_first=()):
            for fg in range(8):
                bank = fg % 2
                w = list(extra_first) if fg == 0 else []
                if fg == 0:
                    w.append((sS, src_ready_s))
                if tr_evac[bank]:
                    w.append((sV, tr_evac[bank]))
                for k in range(4):
                    f = fg * 4 + k
                    P.op("tensor", w if k == 0 else [],
                         lambda e, f=f, k=k, bank=bank: e.matmul(
                             B(bank)[:, k * 128:(k + 1) * 128],
                             xntok[:, f * 128:(f + 1) * 128], iden[:],
                             start=True, stop=True),
                         inc=sT if k == 3 else None)
                t_done = sT.n
                dst = dst_buf[:].rearrange("p (c t) -> p c t", t=SL)[
                    :, 4 * fg:4 * fg + 4, tt * 128:(tt + 1) * 128]
                P.op("vector", [(sT, t_done)],
                     lambda e, d=dst, bank=bank: e.tensor_copy(
                         d, B(bank)[:].rearrange("p (c t) -> p c t", t=128)), inc=sV)
                tr_evac[bank] = sV.n
            return sT.n

        def rope(dst_ap, bank, tcol, t_ready, war=()):
            P.op("vector", [(sT, t_ready)] + list(war),
                 lambda e: e.stream_shuffle(rt[0:64, :], B(bank)[64:128, :], IDMASK))
            P.op("vector", [], lambda e: e.stream_shuffle(rt[64:128, :], B(bank)[0:64, :],
                                                          IDMASK))
            P.op("vector", [], lambda e: e.tensor_tensor(
                t1[:], B(bank)[:], tabc[:, 0:512], ALU.mult))
            P.op("vector", [], lambda e: e.tensor_tensor(
                rt[:], rt[:], tabsg[:, 0:512], ALU.mult))
            P.op("vector", [], lambda e, d=dst_ap: e.tensor_tensor(d, t1[:], rt[:], ALU.add),
                 inc=sV)
            return sV.n

        def chain32(bank, unit_dram, rhs_buf, first_waits, last_inc=True, pre=None):
            if pre is not None:
                i1, i2 = pre
            else:
                i1 = stream(unit_dram[:, 0:2048])
                i2 = stream(unit_dram[:, 2048:4096])
            for c in range(32):
                ii, off = (i1, c) if c < 16 else (i2, c - 16)
                w = []
                if c == 0:
                    w = list(first_waits) + [(sW, slot_ready[i1])]
                elif c == 16:
                    w = [(sW, slot_ready[i2])]
                P.op("tensor", w, lambda e, ii=ii, off=off, c=c, bank=bank: e.matmul(
                    B(bank), slot_ap(ii)[:, off * 128:(off + 1) * 128],
                    rhs_buf[:, c * SL:c * SL + 512],
                    start=(c == 0), stop=(c == 31 and last_inc)),
                    inc=sT if (c == 31 and last_inc) else None)
                if c == 15:
                    slot_guard[i1] = (sT, sT.n + 1)
                if c == 31:
                    slot_guard[i2] = (sT, sT.n if last_inc else sT.n + 1)
            return sT.n

        prev_slice_v = 0
        prev_out_o = 0
        ctx_evac_v = 0
        psT_guard = [0, 0]
        attnT_guard4 = [0, 0, 0]
        evacmult_v = [0]
        emT_evac_v = [0]
        qread_slot = [0, 0]
        s_emf_prev = [0]
        last_rope_v = [0]

        for s in range(NS):
            t0 = s * SL
            nkc = 4 * (s + 1)

            prev_tr_t = 0
            for tt in range(NT):
                g = s * NT + tt
                P.op("gpsimd", [(sV, prev_slice_v)] if tt == 0 else [],
                     lambda e, g=g, tt=tt: e.dma_start(
                         BX[:, tt * E:(tt + 1) * E], x_d[g * 128:(g + 1) * 128, :]),
                     inc=sW, inc_by=16)
                xw = sW.n
                s_xn = ln_tok(BX[:, tt * E:(tt + 1) * E], [(sW, xw)],
                              [(sT, prev_tr_t)] if prev_tr_t else [])
                prev_tr_t = transpose_tile_to_fm(
                    BN, tt, s_xn, extra_first=[(sO, prev_out_o)] if tt == 0 else [])
            xn_ready = sV.n

            P.op("sync", [(sV, last_rope_v[0])] if last_rope_v[0] else [],
                 lambda e, t0=t0: e.dma_start(tabc[:], tabc_d[:, t0:t0 + 512]),
                 inc=sX, inc_by=16)
            P.op("sync", [], lambda e, t0=t0: e.dma_start(
                tabsg[:], tabs_d[:, t0:t0 + 512]), inc=sX, inc_by=16)
            tab_x = sX.n
            kv_rope = {}
            for d in range(NKV):
                bank = 2 + d % 2
                fw = [(sV, xn_ready)] if d == 0 else []
                if d >= 2:
                    fw.append((sV, kv_rope[d - 2]))
                t_k = chain32(bank, wk_d[d], BN, fw)
                kv_rope[d] = rope(Ksb[:, d * T + t0:d * T + t0 + 512], bank, t0, t_k,
                                  war=[(sX, tab_x)] if d == 0 else [])
            q0_pre = (stream(wq_d[0][:, 0:2048]), stream(wq_d[0][:, 2048:4096]))
            vslot_guard = [(sT, 0), (sT, 0)]
            for c in range(32):
                vi = c % 2
                P.op("gpsimd", [vslot_guard[vi]],
                     lambda e, c=c, vi=vi: e.dma_start(
                         vslot[:, vi * 1024:(vi + 1) * 1024], wv_d[c]),
                     inc=sW, inc_by=16)
                v_w = sW.n
                for tb in range(NT):
                    for half in range(2):
                        w = []
                        if tb == 0 and half == 0:
                            w.append((sW, v_w))
                            if c == 0:
                                w.append((sV, xn_ready))
                        if c == 0 and tb == 1 and half == 0:
                            w += [(sV, kv_rope[6]), (sV, kv_rope[7])]
                        P.op("tensor", w,
                             lambda e, c=c, tb=tb, vi=vi, half=half: e.matmul(
                                 ps[:, tb * 1024 + half * 512:
                                    tb * 1024 + (half + 1) * 512],
                                 BN[:, c * SL + tb * 128:c * SL + (tb + 1) * 128],
                                 vslot[:, vi * 1024 + half * 512:
                                       vi * 1024 + (half + 1) * 512],
                                 start=(c == 0), stop=(c == 31),
                                 skip_group_check=True),
                             inc=sT if (tb == NT - 1 and half == 1) else None)
                vslot_guard[vi] = (sT, sT.n)
            t_vmm = sT.n
            for tb in range(NT):
                dst = Vsb[:, (s * NT + tb) * 1024:(s * NT + tb + 1) * 1024]
                if tb < 2:
                    P.op("scalar", [(sT, t_vmm)] if tb == 0 else [],
                         lambda e, dst=dst, tb=tb: e.activation(
                             dst, ps[:, tb * 1024:(tb + 1) * 1024], AF.Identity),
                         inc=sS)
                else:
                    P.op("vector", [(sT, t_vmm)] if tb == 2 else [],
                         lambda e, dst=dst, tb=tb: e.tensor_copy(
                             dst, ps[:, tb * 1024:(tb + 1) * 1024]), inc=sV)
            vevac_s = sS.n
            vevac_v = sV.n

            if phases < 3:
                continue
            exp_tok = {}
            grp = [list(range((nkc * j) // 4, (nkc * (j + 1)) // 4))
                   for j in range(4)]

            q_t = chain32(7, wq_d[0], BN, [(sV, vevac_v), (sS, vevac_s)],
                          pre=q0_pre)
            rope_q = {0: rope(qsb[:, 0:512], 7, t0, q_t,
                              war=[(sT, qread_slot[0])] if qread_slot[0] else [])}
            for h in range(NH):
                kv = h // GQ
                last_stc = 0
                ctx_last = 0
                pend_ctx = None

                def emit_ctx(c, slot, coloff, exp_c, kv=None):
                    nonlocal ctx_last
                    wx = [(sS, exp_c)]
                    if c == 0:
                        if evacmult_v[0]:
                            wx.append((sV, evacmult_v[0]))
                        if h == 0:
                            wx.append((sV, vevac_v))
                    P.op("tensor", wx,
                         lambda e, c=c, kv=kv, slot=slot, coloff=coloff,
                         nkc=nkc: e.matmul(
                             B(6)[:, coloff:512],
                             Vsb[:, c * 1024 + kv * 128:c * 1024 + (kv + 1) * 128],
                             attnT[:, slot * 512 + coloff:(slot + 1) * 512],
                             start=(c == 0), stop=(c == nkc - 1),
                             skip_group_check=True), inc=sT)
                    attnT_guard4[slot] = sT.n
                    ctx_last = sT.n

                for j in range(4):
                    valid = t0 + (j + 1) * 128
                    nch = (valid + 511) // 512
                    if j == 0:
                        w = [(sV, rope_q[h]), (sV, kv_rope[kv])]
                        if s_emf_prev[0]:
                            w.append((sS, s_emf_prev[0]))
                        if emT_evac_v[0]:
                            w.append((sV, emT_evac_v[0]))
                        if h == 0:
                            w += [(sV, vevac_v), (sS, vevac_s)]
                    else:
                        w = [(sS, exp_tok[(h, j - 1)])]
                    for kc in range(nch):
                        k1 = min(valid, (kc + 1) * 512)
                        P.op("tensor", w if kc == 0 else [],
                             lambda e, kc=kc, k1=k1, kv=kv, j=j, h=h: e.matmul(
                                 ps[:, kc * 512:k1],
                                 qsb[:, (h % 2) * 512 + j * 128:
                                     (h % 2) * 512 + (j + 1) * 128],
                                 Ksb[:, kv * T + kc * 512:kv * T + k1],
                                 start=True, stop=True))
                    P.op("tensor", [], lambda e, valid=valid: e.matmul(
                        ps[:, valid - 128:valid], iden[:], maskc[:],
                        start=False, stop=True, skip_group_check=True), inc=sT)
                    t_sc = sT.n
                    P.op("vector", [(sT, t_sc)], lambda e, valid=valid, j=j:
                         e.reduce_max(nmx[:, j:j + 1], ps[:, 0:valid], axis=AX.X))
                    P.op("vector", [], lambda e: e.tensor_copy(jkv[:], rstd[:]))
                    P.op("vector", [], lambda e, j=j: e.tensor_scalar_mul(
                        nmx[:, j:j + 1], nmx[:, j:j + 1], -INVSQ), inc=sV)
                    v_mx = sV.n
                    P.op("scalar", [(sV, v_mx)],
                         lambda e, valid=valid, j=j: e.activation(
                             escr[:, 0:valid], ps[:, 0:valid], AF.Exp,
                             bias=nmx[:, j:j + 1], scale=INVSQ,
                             accum_out=lsum[:, j:j + 1]), inc=sS)
                    exp_tok[(h, j)] = sS.n
                    for c in grp[j]:
                        d = c - 4 * s
                        coloff = max(0, d) * 128
                        pb = 4 + (c & 1)
                        diag = d >= 0
                        wc = []
                        if psT_guard[c & 1]:
                            wc.append((sS, psT_guard[c & 1]))
                        if c == 0 and evacmult_v[0]:
                            wc.append((sV, evacmult_v[0]))
                        if h == 0 and c <= 1:
                            wc.append((sV, vevac_v))
                        P.op("tensor", wc,
                             lambda e, pb=pb, coloff=coloff, c=c, kv=kv, h=h,
                             diag=diag: e.matmul(
                                 ps[:, pb * 512 + coloff:(pb + 1) * 512],
                                 Ksb[:, kv * T + c * 128:kv * T + (c + 1) * 128],
                                 qsb[:, (h % 2) * 512 + coloff:(h % 2) * 512 + 512],
                                 start=True, stop=not diag, skip_group_check=True),
                             inc=None if diag else sT)
                        if diag:
                            P.op("tensor", [],
                                 lambda e, pb=pb, coloff=coloff: e.matmul(
                                     ps[:, pb * 512 + coloff:pb * 512 + coloff + 128],
                                     iden[:], maskr[:], start=False, stop=True,
                                     skip_group_check=True), inc=sT)
                        t_stc = sT.n
                        last_stc = t_stc
                        slot = c % 3
                        we = [(sT, t_stc)]
                        if attnT_guard4[slot]:
                            we.append((sT, attnT_guard4[slot]))
                        P.op("scalar", we,
                             lambda e, slot=slot, coloff=coloff, pb=pb: e.activation(
                                 attnT[:, slot * 512 + coloff:(slot + 1) * 512],
                                 ps[:, pb * 512 + coloff:(pb + 1) * 512], AF.Exp,
                                 scale=INVSQ), inc=sS)
                        exp_c = sS.n
                        psT_guard[c & 1] = exp_c
                        if pend_ctx is not None:
                            emit_ctx(*pend_ctx, kv=kv)
                        pend_ctx = (c, slot, coloff, exp_c)
                if pend_ctx is not None:
                    emit_ctx(*pend_ctx, kv=kv)
                    pend_ctx = None
                qread_slot[h % 2] = last_stc
                P.op("vector", [(sS, exp_tok[(h, 3)])], lambda e: e.reciprocal(
                    rlsum[:, 0:4], lsum[:, 0:4]))
                P.op("vector", [], lambda e: e.tensor_copy(jkv[:], rstd[:]))
                P.op("scalar", [], lambda e: e.activation(
                    emf[:, 0:4], nmx[:, 0:4], AF.Exp), inc=sS)
                s_emf_prev[0] = sS.n
                P.op("vector", [(sS, s_emf_prev[0])], lambda e: e.tensor_tensor(
                    emf[:, 0:4], emf[:, 0:4], rlsum[:, 0:4], ALU.mult), inc=sV)
                v_emf = sV.n
                if h + 1 < NH:
                    q_t = chain32(7, wq_d[h + 1], BN, [(sV, rope_q[h])])
                for jj in range(4):
                    wb = [(sV, v_emf), (sS, exp_tok[(h, 3)])] if jj == 0 else []
                    P.op("tensor", wb, lambda e, jj=jj: e.matmul(
                        ps[0:1, jj * 128:(jj + 1) * 128], emf[:, jj:jj + 1],
                        iden[:], start=True, stop=True),
                        inc=sT if jj == 3 else None)
                t_emt = sT.n
                P.op("vector", [(sT, t_emt)], lambda e: e.tensor_copy(
                    emTrow, ps[0:1, 0:512]), inc=sV)
                emT_evac_v[0] = sV.n
                wb = [(sV, emT_evac_v[0])]
                if psT_guard[0]:
                    wb.append((sS, psT_guard[0]))
                P.op("tensor", wb, lambda e: e.matmul(
                    ps[:, 4 * 512:5 * 512], ones1[:], emTrow,
                    start=True, stop=True), inc=sT)
                t_bc = sT.n
                psT_guard[0] = 0
                P.op("vector", [(sT, t_bc)], lambda e: e.tensor_copy(
                    embc_sb[:], ps[:, 4 * 512:5 * 512]))
                P.op("vector", [(sT, ctx_last)], lambda e, h=h: e.tensor_tensor(
                    BC[:, h * 512:(h + 1) * 512], B(6), embc_sb[:], ALU.mult),
                    inc=sV)
                evacmult_v[0] = sV.n
                if h + 1 < NH:
                    rope_q[h + 1] = rope(
                        qsb[:, ((h + 1) % 2) * 512:((h + 1) % 2) * 512 + 512],
                        7, t0, q_t,
                        war=[(sT, qread_slot[(h + 1) % 2])]
                        if qread_slot[(h + 1) % 2] else [])
            ctx_evac_v = evacmult_v[0]
            last_rope_v[0] = rope_q[NH - 1]

            pp_ready = [0]

            def emit_gates(xn2_v):
                for c in range(32):
                    w = [(sV, xn2_v), (sW, init_w)] if c == 0 else []
                    P.op("tensor", w, lambda e, c=c: e.matmul(
                        ps[0:32, 7 * 512:8 * 512], asb[:, c * 36:c * 36 + 32],
                        BN[:, c * SL:c * SL + 512], start=(c == 0), stop=(c == 31)))
                    P.op("tensor", [], lambda e, c=c: e.matmul(
                        ps[0:4, 3 * 512:4 * 512], asb[:, c * 36 + 32:c * 36 + 36],
                        BN[:, c * SL:c * SL + 512], start=(c == 0), stop=(c == 31)),
                        inc=sT if c == 31 else None)
                t_pg = sT.n
                P.op("vector", [(sT, t_pg)], lambda e: e.tensor_copy(
                    glogsb[:], ps[0:4, 3 * 512:4 * 512]), inc=sV)
                v_gl = sV.n
                for k in range(4):
                    P.op("tensor", [(sV, v_gl), (sX, init_x)] if k == 0 else [],
                         lambda e, k=k: e.matmul(
                             B(5)[:, k * 4:(k + 1) * 4],
                             glogsb[:, k * 128:(k + 1) * 128], iden[0:4, 0:4],
                             start=True, stop=True), inc=sT if k == 3 else None)
                t_gt = sT.n
                for k in range(4):
                    P.op("vector", [(sT, t_gt)] if k == 0 else [],
                         lambda e, k=k: e.reduce_max(
                             gl1[:], B(5)[:, k * 4:(k + 1) * 4], axis=AX.X, negate=True),
                         inc=sV)
                    v_m = sV.n
                    P.op("scalar", [(sV, v_m)], lambda e, k=k: e.activation(
                        gtok[:, k * 4:(k + 1) * 4], B(5)[:, k * 4:(k + 1) * 4], AF.Exp,
                        bias=gl1[:]), inc=sS)
                    s_e = sS.n
                    P.op("vector", [(sS, s_e)], lambda e, k=k: e.reduce_sum(
                        gl2[:], gtok[:, k * 4:(k + 1) * 4], axis=AX.X))
                    P.op("vector", [], lambda e: e.tensor_copy(jkv[:], rstd[:]))
                    P.op("vector", [], lambda e: e.reciprocal(gl1[:], gl2[:]))
                    P.op("vector", [], lambda e: e.tensor_copy(jkv[:], rstd[:]))
                    P.op("vector", [], lambda e, k=k: e.tensor_scalar_mul(
                        gtok[:, k * 4:(k + 1) * 4], gtok[:, k * 4:(k + 1) * 4], gl1[:]),
                        inc=sV)
                v_gn = sV.n
                for k in range(4):
                    P.op("tensor", [(sV, v_gn)] if k == 0 else [],
                         lambda e, k=k: e.matmul(
                             ps[0:4, 2 * 512 + k * 128:2 * 512 + (k + 1) * 128],
                             gtok[:, k * 4:(k + 1) * 4], iden[:],
                             start=True, stop=True), inc=sT if k == 3 else None)
                t_gb = sT.n
                P.op("vector", [(sT, t_gb)], lambda e: e.tensor_scalar_mul(
                    glogsb[:], ps[0:4, 2 * 512:3 * 512], LORA_SCALE), inc=sV)
                v_g2 = sV.n
                P.op("tensor", [(sV, v_g2)], lambda e: e.matmul(
                    ps[0:32, 2 * 512:3 * 512], e4[:], glogsb[:], start=True, stop=True),
                    inc=sT)
                t_gbb = sT.n
                P.op("vector", [(sT, t_gbb)], lambda e: e.tensor_copy(
                    gbbsb[:], ps[0:32, 2 * 512:3 * 512]), inc=sV)
                v_gbb = sV.n
                P.op("vector", [(sV, v_gbb)], lambda e: e.tensor_tensor(
                    Ppsb[0:32, :], ps[0:32, 7 * 512:8 * 512], gbbsb[:], ALU.mult), inc=sV)
                pp_ready[0] = sV.n

            out_wait = {}

            def proj_ln_residual(w_dram, src_buf, y_buf, is_ffn, first_v):
                nonlocal prev_out_o
                yev = {0: 0, 1: 0}
                pend_stats = []

                def emit_stats(oc, v_ysq):
                    P.op("tensor", [(sV, v_ysq)], lambda e, oc=oc: e.matmul(
                        B(2)[0:1, :], onesp[:], y_buf[:, oc * 512:(oc + 1) * 512],
                        start=(oc == 0), stop=(oc == 31)))
                    P.op("tensor", [], lambda e, oc=oc: e.matmul(
                        B(3)[0:1, :], onesp[:], ysq[:, (oc % 2) * 512:(oc % 2) * 512 + 512],
                        start=(oc == 0), stop=(oc == 31)), inc=sT)

                for oc in range(32):
                    bank = oc % 2
                    fw = []
                    if oc == 0:
                        fw.append((sV, first_v))
                    if yev[bank]:
                        fw.append((sS, yev[bank]))
                    t_y = chain32(bank, w_dram[oc], src_buf, fw, last_inc=not is_ffn)
                    if is_ffn:
                        P.op("tensor", [(sV, pp_ready[0])] if oc == 0 else [],
                             lambda e, oc=oc, bank=bank: e.matmul(
                                 B(bank), bbsb[:, oc * 128:(oc + 1) * 128], Ppsb[:],
                                 start=False, stop=True, skip_group_check=True), inc=sT)
                        t_y = sT.n
                    P.op("scalar", [(sT, t_y)], lambda e, oc=oc, bank=bank: e.activation(
                        y_buf[:, oc * 512:(oc + 1) * 512], B(bank), AF.Identity), inc=sS)
                    yev[bank] = sS.n
                    wq_ = [(sS, yev[bank])]
                    P.op("vector", wq_, lambda e, oc=oc: e.tensor_tensor(
                        ysq[:, (oc % 2) * 512:(oc % 2) * 512 + 512],
                        y_buf[:, oc * 512:(oc + 1) * 512],
                        y_buf[:, oc * 512:(oc + 1) * 512], ALU.mult), inc=sV)
                    pend_stats.append((oc, sV.n))
                    if len(pend_stats) >= 2:
                        emit_stats(*pend_stats.pop(0))
                while pend_stats:
                    emit_stats(*pend_stats.pop(0))
                stat_t = sT.n
                P.op("vector", [(sT, stat_t)], lambda e: e.tensor_scalar_mul(
                    rowa, B(2)[0:1, :], 1.0 / 4096.0))
                P.op("vector", [], lambda e: e.tensor_scalar_mul(
                    rowb, B(3)[0:1, :], 1.0 / 4096.0))
                P.op("vector", [], lambda e: e.tensor_tensor(
                    rowt, rowa, rowa, ALU.mult))
                P.op("vector", [], lambda e: e.tensor_sub(rowb, rowb, rowt))
                P.op("vector", [], lambda e: e.tensor_scalar_add(rowb, rowb, EPS),
                     inc=sV)
                v_var = sV.n
                P.op("scalar", [(sV, v_var)], lambda e: e.activation(rowb, rowb, AF.Ln))
                P.op("scalar", [], lambda e: e.activation(rowr, rowb, AF.Exp,
                                                          scale=-0.5), inc=sS)
                s_r = sS.n
                P.op("vector", [(sS, s_r)], lambda e: e.tensor_tensor(
                    rowmr, rowa, rowr, ALU.mult), inc=sV)
                v_mr = sV.n
                P.op("tensor", [(sV, v_mr)], lambda e: e.matmul(
                    B(4), ones1[:], rowr, start=True, stop=True))
                P.op("tensor", [], lambda e: e.matmul(
                    B(5), ones1[:], rowmr, start=True, stop=True), inc=sT)
                t_bc = sT.n
                P.op("vector", [(sT, t_bc)], lambda e: e.tensor_copy(rbc[:], B(4)))
                P.op("vector", [], lambda e: e.tensor_copy(mrbc[:], B(5)), inc=sV)
                v_bc = sV.n
                ln_done = {}
                for ocg in range(8):
                    for k in range(4):
                        oc = ocg * 4 + k
                        P.op("vector", [(sV, v_bc)] if oc == 0 else [],
                             lambda e, oc=oc: e.tensor_tensor(
                                 y_buf[:, oc * 512:(oc + 1) * 512],
                                 y_buf[:, oc * 512:(oc + 1) * 512], rbc[:], ALU.mult))
                        P.op("vector", [], lambda e, oc=oc: e.tensor_tensor(
                            y_buf[:, oc * 512:(oc + 1) * 512],
                            y_buf[:, oc * 512:(oc + 1) * 512], mrbc[:], ALU.subtract),
                            inc=sV if k == 3 else None)
                    ln_done[ocg] = sV.n
                ep_ev = {6: 0, 7: 0}
                for tt in range(NT):
                    for ocg in range(8):
                        bank = 6 + (ocg % 2)
                        w = [(sV, ln_done[ocg])] if tt == 0 else []
                        if ep_ev[bank]:
                            w.append((sV, ep_ev[bank]))
                        for k in range(4):
                            oc = ocg * 4 + k
                            P.op("tensor", w if k == 0 else [],
                                 lambda e, oc=oc, k=k, tt=tt, bank=bank: e.matmul(
                                     B(bank)[:, k * 128:(k + 1) * 128],
                                     y_buf[:, oc * 512 + tt * 128:oc * 512 + tt * 128 + 128],
                                     iden[:], start=True, stop=True),
                                 inc=sT if k == 3 else None)
                        t_ep = sT.n
                        if not is_ffn:
                            dst = BX[:, tt * E + ocg * 512:tt * E + (ocg + 1) * 512]
                            P.op("vector", [(sT, t_ep)], lambda e, d=dst, bank=bank:
                                 e.tensor_tensor(d, B(bank), d, ALU.add), inc=sV)
                        else:
                            stg = BN_f32[:, (tt % 2) * 4096 + ocg * 512:
                                         (tt % 2) * 4096 + (ocg + 1) * 512]
                            xr = BX[:, tt * E + ocg * 512:tt * E + (ocg + 1) * 512]
                            w2 = [(sT, t_ep)]
                            if tt >= 2 and ocg == 0:
                                w2.append((sO, out_wait[tt - 2]))
                            P.op("vector", w2, lambda e, stg=stg, xr=xr, bank=bank:
                                 e.tensor_tensor(stg, B(bank), xr, ALU.add), inc=sV)
                        ep_ev[bank] = sV.n
                    if is_ffn:
                        v_tt = sV.n
                        g = s * NT + tt
                        P.op("sync", [(sV, v_tt)], lambda e, g=g, tt=tt: e.dma_start(
                            out_d[g * 128:(g + 1) * 128, :],
                            BN_f32[:, (tt % 2) * 4096:(tt % 2) * 4096 + 4096]),
                            inc=sO, inc_by=16)
                        out_wait[tt] = sO.n
                if is_ffn:
                    prev_out_o = sO.n

            if phases < 4:
                continue
            proj_ln_residual(wo_d, BC, BN, is_ffn=False, first_v=ctx_evac_v)
            x_prime_v = sV.n

            if phases < 5:
                continue
            prev_tr_t = 0
            for tt in range(NT):
                s_xn = ln_tok(BX[:, tt * E:(tt + 1) * E],
                              [(sV, x_prime_v)] if tt == 0 else [],
                              [(sT, prev_tr_t)] if prev_tr_t else [])
                prev_tr_t = transpose_tile_to_fm(BN, tt, s_xn)
            xn2_ready = sV.n

            if phases < 6:
                continue
            emit_gates(xn2_ready)
            proj_ln_residual(wf_d, BN, BC, is_ffn=True, first_v=pp_ready[0])
            prev_slice_v = sV.n

        if phases < 6:
            P.op("sync", [(sV, sV.n), (sS, sS.n)],
                 lambda e: e.dma_start(out_d[0:128, :], BX[:].bitcast(F32)[:, 0:4096]),
                 inc=sO, inc_by=16)
        final_o = sO.n

        @block.sync
        def _(e):
            P.replay("sync", e)
            e.wait_ge(_sO, final_o)

        @block.gpsimd
        def _(e):
            P.replay("gpsimd", e)

        @block.tensor
        def _(e):
            P.replay("tensor", e)

        @block.vector
        def _(e):
            P.replay("vector", e)

        @block.scalar
        def _(e):
            P.replay("scalar", e)

    return nc



def _rope_tables(T):
    inv_freq = 1.0 / (500000.0 ** (np.arange(0, 128, 2, dtype=np.float32) / 128.0))
    wavelen = 2.0 * math.pi / inv_freq
    cut_lo = 8192.0 / 4.0
    cut_hi = 8192.0 / 1.0
    inv_freq = np.where(wavelen > cut_hi, inv_freq / 32.0, inv_freq).astype(np.float32)
    smooth = ((8192.0 / wavelen - 1.0) / 3.0).astype(np.float32)
    smoothed = ((1.0 - smooth) * inv_freq / 32.0 + smooth * inv_freq).astype(np.float32)
    is_medium = (~(wavelen < cut_hi)) & (~(wavelen > cut_lo))
    inv_freq = np.where(is_medium, smoothed, inv_freq).astype(np.float32)
    pos = np.arange(T, dtype=np.float32)
    freqs = pos[:, None] * inv_freq[None, :]
    cos = np.cos(freqs).astype(np.float32).T
    sin = np.sin(freqs).astype(np.float32).T
    tabc = np.concatenate([cos, cos], 0).astype(np.float16)
    tabsg = np.concatenate([-sin, sin], 0).astype(np.float16)
    return tabc, tabsg


def _retile(WT):
    I, O = WT.shape
    return np.ascontiguousarray(
        WT.reshape(I // 128, 128, O // 128, 128).transpose(2, 1, 0, 3).reshape(
            O // 128, 128, I))


def prep_inputs(x, Wq, Wk, Wv, Wo, W, b, A, Bm, Wr, T):
    wq = _retile(np.ascontiguousarray(Wq.T))
    wk = _retile(np.ascontiguousarray(Wk.T))
    wv = np.ascontiguousarray(Wv.T).reshape(32, 128, 1024)
    wo = _retile(np.ascontiguousarray(Wo.T))
    wf = _retile(np.ascontiguousarray(W.T))
    A2 = A.reshape(32, 4096)
    AW = np.concatenate([A2, Wr], 0).T
    ab = np.ascontiguousarray(AW.reshape(32, 128, 36).transpose(1, 0, 2).reshape(128, 1152))
    bb = np.ascontiguousarray(
        np.concatenate([Bm.transpose(0, 2, 1).reshape(32, 4096), b[None, :]], 0))
    tabc, tabsg = _rope_tables(T)
    iden = np.eye(128, dtype=np.float32).astype(ml_dtypes.bfloat16)
    maskc = np.triu(np.full((128, 128), -1e38, dtype=np.float32), k=1).astype(
        ml_dtypes.bfloat16)
    maskr = np.tril(np.full((128, 128), -1e38, dtype=np.float32), k=-1).astype(
        ml_dtypes.bfloat16)
    e4 = np.zeros((4, 32), dtype=np.float32)
    for i in range(4):
        e4[i, i * 8:(i + 1) * 8] = 1.0
    shared = dict(
        wq=np.ascontiguousarray(wq, np.float32), wk=np.ascontiguousarray(wk, np.float32),
        wv=np.ascontiguousarray(wv, np.float32), wo=np.ascontiguousarray(wo, np.float32),
        wf=np.ascontiguousarray(wf, np.float32), ab=np.ascontiguousarray(ab, np.float32),
        bb=np.ascontiguousarray(bb, np.float32), tabc=tabc, tabs=tabsg,
        iden=iden, maskc=maskc, maskr=maskr, e4=e4.astype(ml_dtypes.bfloat16),
        ones1=np.ones((1, 128), np.float32).astype(ml_dtypes.bfloat16),
        onesp=np.ones((128, 1), np.float32).astype(ml_dtypes.bfloat16))
    in_maps = []
    for bi in range(x.shape[0]):
        m = dict(shared)
        m["x"] = np.ascontiguousarray(x[bi], np.float32)
        in_maps.append(m)
    return in_maps


def kernel(L0_attn_x, L0_attn_Wq, L0_attn_Wk, L0_attn_Wv, L0_attn_Wo,
           L0_ffn_W, L0_ffn_b, L0_ffn_A, L0_ffn_Bm, L0_ffn_Wr):
    x = np.asarray(L0_attn_x, np.float32)
    Bn, T, _ = x.shape
    in_maps = prep_inputs(
        x, np.asarray(L0_attn_Wq, np.float32), np.asarray(L0_attn_Wk, np.float32),
        np.asarray(L0_attn_Wv, np.float32), np.asarray(L0_attn_Wo, np.float32),
        np.asarray(L0_ffn_W, np.float32), np.asarray(L0_ffn_b, np.float32),
        np.asarray(L0_ffn_A, np.float32), np.asarray(L0_ffn_Bm, np.float32),
        np.asarray(L0_ffn_Wr, np.float32), T)
    nc = build(T=T)
    res = run_bass_kernel_spmd(nc, in_maps, core_ids=list(range(Bn)))
    return np.stack([r["out"] for r in res.results], axis=0).astype(np.float32)
